# version 28
# baseline (speedup 1.0000x reference)
import numpy as np
from contextlib import ExitStack

import concourse.bass as bass
import concourse.mybir as mybir
from concourse.bass_utils import run_bass_kernel_spmd

F32 = mybir.dt.float32
BF16 = mybir.dt.bfloat16
AF = mybir.ActivationFunctionType
ALU = mybir.AluOpType
AX = mybir.AxisListType

NCORES = 8
SEQ = 2
L = 2048
DM = 1024
NH = 16
HD = 64
NE = 8
NT = 16
NSL = 4
NCH = 8

GOLD_C = float(1.0 / np.e)
GOLD_L = float(0.5 - np.log(4.0 / 3.0))
GOLD_U = 0.5

DEBUG = ()

ENGINES = ("sync", "tensor", "scalar", "vector", "gpsimd")
AUTOCHAIN = ("vector",)


class Sched:

    def __init__(self):
        self.prog = {e: [] for e in ENGINES}
        self.cnt = {"T": 0, "S": 0, "V": 0, "G": 0, "DS": 0, "DG": 0}
        self.seen = {e: {} for e in ENGINES}
        self.esem = {"tensor": "T", "scalar": "S", "vector": "V", "gpsimd": "G"}
        self.qsem = {"sync": "DS", "gpsimd": "DG"}
        self.pending = {e: [] for e in ENGINES}
        self.lasttok = {e: None for e in ENGINES}
        self.h = None

    def _waits(self, eng, deps):
        out = []
        seen = self.seen[eng]
        alldeps = list(self.pending[eng]) + [d for d in deps if d is not None]
        self.pending[eng] = []
        for sem, val in alldeps:
            if eng == "tensor" and sem == "T":
                continue
            if seen.get(sem, 0) >= val:
                continue
            seen[sem] = val
            out.append((sem, val))
        return out

    def op(self, eng, fn, deps=(), inc=True):
        deps = list(deps)
        if eng in AUTOCHAIN:
            deps.append(self.lasttok[eng])
            inc = True
        waits = self._waits(eng, deps)
        sem = self.esem[eng]
        tok = None
        if inc:
            self.cnt[sem] += 1
            tok = (sem, self.cnt[sem])
            self.lasttok[eng] = tok

        def emit(e, waits=waits, fn=fn, inc=inc, sem=sem, sched=self):
            for s, val in waits:
                e.wait_ge(sched.h[s], val)
            ins = fn(e)
            if inc:
                ins.then_inc(sched.h[sem], 1)
        self.prog[eng].append(emit)
        return tok

    def dma(self, eng, out, in_, deps=()):
        waits = self._waits(eng, deps)
        q = self.qsem[eng]
        self.cnt[q] += 16
        tok = (q, self.cnt[q])

        def emit(e, waits=waits, out=out, in_=in_, q=q, sched=self):
            for s, val in waits:
                e.wait_ge(sched.h[s], val)
            e.dma_start(out=out, in_=in_).then_inc(sched.h[q], 16)
        self.prog[eng].append(emit)
        return tok

    def ttok(self):
        return ("T", self.cnt["T"])

    def barrier(self):
        f = [(s, v) for s, v in self.cnt.items() if v > 0]
        for e in ENGINES:
            self.pending[e].extend(f)

    def wait_raw(self, eng, deps):
        waits = self._waits(eng, deps)

        def emit(e, waits=waits, sched=self):
            for s, val in waits:
                e.wait_ge(sched.h[s], val)
        self.prog[eng].append(emit)

    def finalize(self, block, handles):
        self.h = handles
        for eng in ENGINES:
            lst = self.prog[eng]
            if not lst:
                continue
            sec = getattr(block, eng)

            def runner(e, lst=lst):
                for f in lst:
                    f(e)
            sec(runner)


class Arena:

    def __init__(self, hb, nb, hf, nf):
        self.hb, self.nb = hb, nb
        self.hf, self.nf = hf, nf
        self.ob = 0
        self.of = 0

    def bf16(self, rows, cols):
        cols_a = (cols + 15) & ~15
        assert self.ob + cols_a <= self.nb, "bf16 arena overflow"
        o = self.ob
        self.ob += cols_a
        return self.hb[0:rows, o:o + cols]

    def f32(self, rows, cols):
        cols_a = (cols + 7) & ~7
        assert self.of + cols_a <= self.nf, "f32 arena overflow"
        o = self.of
        self.of += cols_a
        return self.hf[0:rows, o:o + cols]

    def mark(self):
        return (self.ob, self.of)

    def reset(self, m):
        self.ob, self.of = m


def build_program():
    nc = bass.Bass("TRN2", target_bir_lowering=False)

    d_xtm = nc.dram_tensor("xtm", [SEQ, L, DM], F32, kind="ExternalInput").ap()
    d_xfm = nc.dram_tensor("xfm", [SEQ, DM, L], F32, kind="ExternalInput").ap()
    d_wqT = nc.dram_tensor("wqT", [DM, DM], F32, kind="ExternalInput").ap()
    d_wkT = nc.dram_tensor("wkT", [DM, DM], F32, kind="ExternalInput").ap()
    d_wvT = nc.dram_tensor("wvT", [DM, DM], F32, kind="ExternalInput").ap()
    d_woT = nc.dram_tensor("woT", [DM, DM], F32, kind="ExternalInput").ap()
    d_b = {k: nc.dram_tensor("b" + k, [1, DM], F32, kind="ExternalInput").ap()
           for k in "qkvo"}
    d_wg = nc.dram_tensor("wg", [NE, DM, DM], F32, kind="ExternalInput").ap()
    d_wu = nc.dram_tensor("wu", [NE, DM, DM], F32, kind="ExternalInput").ap()
    d_wd = nc.dram_tensor("wd", [NE, DM, DM], F32, kind="ExternalInput").ap()
    d_wr = nc.dram_tensor("wr", [DM, NE], F32, kind="ExternalInput").ap()
    d_tbc = nc.dram_tensor("tbc", [128, 1], F32, kind="ExternalInput").ap()
    d_cfm = nc.dram_tensor("cfm", [128, L], F32, kind="ExternalInput").ap()
    d_sfm = nc.dram_tensor("sfm", [128, L], F32, kind="ExternalInput").ap()
    d_ident = nc.dram_tensor("ident", [128, 128], F32, kind="ExternalInput").ap()
    d_perm = nc.dram_tensor("permm", [128, 128], F32, kind="ExternalInput").ap()
    d_oneh = nc.dram_tensor("oneh", [NE, NE * 128], F32, kind="ExternalInput").ap()
    d_out = nc.dram_tensor("out", [SEQ, L, DM], F32, kind="ExternalOutput").ap()
    d_x1tm = nc.dram_tensor("x1tm_scr", [L, DM], F32).ap()
    d_x1hi = nc.dram_tensor("x1hi_scr", [DM, L], BF16).ap()
    d_x1lo = nc.dram_tensor("x1lo_scr", [DM, L], BF16).ap()

    dbg = {}
    for name in DEBUG:
        shape, dt = {
            "qfm": ([128, NCH * L], BF16), "kfm": ([128, NCH * L], BF16),
            "vaug": ([128, NT * NH * 65], BF16),
            "ao": ([128, NCH * 512], BF16),
            "pt": ([128, NT * 1024], BF16),
            "w_tm": ([128, NT * NE], BF16),
            "x1hi": ([DM, L], BF16),
            "x1tmv": ([L, DM], F32),
        }[name]
        dbg[name] = nc.dram_tensor("dbg_" + name, shape, dt,
                                   kind="ExternalOutput").ap()

    sch = Sched()
    with ExitStack() as st:
        sems = {}
        for sname in ("T", "S", "V", "G", "DS", "DG"):
            sems[sname] = st.enter_context(nc.semaphore("sem" + sname))

        identb = st.enter_context(nc.sbuf_tensor("identb", [128, 128], BF16))
        permb = st.enter_context(nc.sbuf_tensor("permb", [128, 128], BF16))
        onehb = st.enter_context(nc.sbuf_tensor("onehb", [NE, NE * 128], BF16))
        ones1 = st.enter_context(nc.sbuf_tensor("ones1", [1, 128], BF16))
        browb = st.enter_context(nc.sbuf_tensor("browb", [1, 4 * DM], BF16))
        wrf = st.enter_context(nc.sbuf_tensor("wrf", [128, NCH * NE], F32))
        whilo = st.enter_context(nc.sbuf_tensor("whilo", [128, NCH * 16], BF16))
        tcol = st.enter_context(nc.sbuf_tensor("tcol", [128, 1], F32))
        invT = st.enter_context(nc.sbuf_tensor("invT", [128, 1], F32))
        w_tm = st.enter_context(nc.sbuf_tensor("w_tm", [128, NT * NE], BF16))
        stats = st.enter_context(nc.sbuf_tensor("stats", [128, 16], F32))
        epscol = st.enter_context(nc.sbuf_tensor("epscol", [128, 1], F32))
        rsc = st.enter_context(nc.sbuf_tensor("rsc", [128, 12 * NE], F32))

        ARB_N = 84480
        ARF_N = 6656
        arena_b = st.enter_context(nc.sbuf_tensor("arenab", [128, ARB_N], BF16))
        arena_f = st.enter_context(nc.sbuf_tensor("arenaf", [128, ARF_N], F32))
        ar = Arena(arena_b, ARB_N, arena_f, ARF_N)

        PA = st.enter_context(nc.psum_tensor("PA", [128, 1024], F32))
        PB = st.enter_context(nc.psum_tensor("PB", [128, 1024], F32))
        PC = st.enter_context(nc.psum_tensor("PC", [128, 1024], F32))
        PD = st.enter_context(nc.psum_tensor("PD", [128, 512], F32))
        PT = st.enter_context(nc.psum_tensor("PT", [128, 1024], BF16))

        block = st.enter_context(nc.Block())

        t_consts = []
        t_consts.append(sch.dma("gpsimd", identb[:], d_ident[:]))
        t_consts.append(sch.dma("gpsimd", permb[:], d_perm[:]))
        t_consts.append(sch.dma("gpsimd", onehb[:], d_oneh[:]))
        sch.op("vector", lambda e: e.memset(ones1[:], 1.0))
        sch.op("vector", lambda e: e.memset(epscol[:], 1e-5))
        for i, key in enumerate("qkvo"):
            t_consts.append(sch.dma(
                "gpsimd", browb[0:1, i * DM:(i + 1) * DM], d_b[key][:]))
        td = sch.dma("sync", wrf[:, :].rearrange("p (c e) -> p c e", e=NE),
                     d_wr.rearrange("(c p) e -> p c e", p=128))
        whi3 = whilo[:, :].rearrange("p (c t) -> p c t", t=16)
        wrf3 = wrf[:, :].rearrange("p (c e) -> p c e", e=NE)
        c_whi = sch.op("scalar", lambda e: e.copy(
            out=whi3[:, :, 0:NE], in_=wrf3), deps=[td])
        sch.op("vector", lambda e: e.tensor_tensor(
            out=whi3[:, :, NE:16], in0=wrf3, in1=whi3[:, :, 0:NE],
            op=ALU.subtract), deps=[td, c_whi])
        td = sch.dma("sync", tcol[:], d_tbc[:])
        sch.op("vector", lambda e: e.reciprocal(out=invT[:], in_=tcol[:]),
               deps=[td])
        sch.barrier()

        out_toks = []
        psfree = {"PT": None}

        for s in range(SEQ):
            m_span = ar.mark()
            q_fm = ar.bf16(128, NCH * L)
            k_fm = ar.bf16(128, NCH * L)
            v_aug = ar.bf16(128, NT * NH * 65)

            m_A = ar.mark()
            x_fm = ar.bf16(128, NCH * L)
            wpool = [ar.bf16(128, NCH * DM), ar.bf16(128, NCH * DM)]
            qtm = ar.bf16(128, 1024)
            ropescr = ar.f32(128, 1024)
            cfm_t = ar.f32(128, L)
            sfm_t = ar.f32(128, L)

            frontier_now = [(s2, v2) for s2, v2 in sch.cnt.items() if v2 > 0]
            t_cf = sch.dma("sync", cfm_t, d_cfm[:], deps=frontier_now)
            t_sf = sch.dma("sync", sfm_t, d_sfm[:])

            va3 = v_aug.rearrange("p (b h c) -> p b h c", h=NH, c=65)
            va_ones = sch.op("vector", lambda e: e.memset(va3[:, :, :, 64:65], 1.0))

            xc_last = sch.dma("gpsimd",
                              x_fm.rearrange("p (c t) -> p c t", t=L),
                              d_xfm[s].rearrange("(c p) t -> p c t", p=128))

            def load_w(dram2d, dst, dep=None):
                return sch.dma("gpsimd",
                               dst.rearrange("p (c i) -> p c i", i=DM),
                               dram2d.rearrange("(c p) i -> p c i", p=128),
                               deps=[dep])

            wq_ok = load_w(d_wqT, wpool[0])
            wk_ok = load_w(d_wkT, wpool[1])

            ps_evac = {"PA": None, "PB": None, "PC": None}
            qtm_free = None
            last_qmm = None

            def proj_tile(P, pname, wslot, wtok, bofs, tt, extra=None):
                deps0 = [xc_last, wtok, ps_evac[pname]]
                if extra:
                    deps0 += extra
                for c in range(NCH):
                    for half in range(2):
                        sch.op("tensor", lambda e, P=P, c=c, half=half, tt=tt, wslot=wslot: e.matmul(
                            out=P[:, half * 512:(half + 1) * 512],
                            lhsT=x_fm[:, c * L + tt * 128: c * L + tt * 128 + 128],
                            rhs=wpool[wslot][:, c * DM + half * 512: c * DM + (half + 1) * 512],
                            start=(c == 0), stop=False),
                            deps=(deps0 if (c == 0 and half == 0) else ()), inc=False)
                mmtok = None
                for half in range(2):
                    mmtok = sch.op("tensor", lambda e, P=P, half=half, bofs=bofs: e.matmul(
                        out=P[:, half * 512:(half + 1) * 512],
                        lhsT=ones1[:],
                        rhs=browb[0:1, bofs * DM + half * 512: bofs * DM + (half + 1) * 512],
                        start=False, stop=True), inc=(half == 1))
                return mmtok

            for tt in range(NT):
                for which, (P, pname, wslot, wtok, bofs, dst) in enumerate((
                        (PA, "PA", 0, wq_ok, 0, q_fm),
                        (PB, "PB", 1, wk_ok, 1, k_fm))):
                    mmtok = proj_tile(P, pname, wslot, wtok, bofs, tt)
                    if which == 0 and tt == NT - 1:
                        last_qmm = mmtok
                    ev = sch.op("scalar", lambda e, P=P: e.copy(out=qtm[:], in_=P[:]),
                                deps=[mmtok, qtm_free])
                    ps_evac[pname] = ev
                    ttr = None
                    for c in range(NCH):
                        ttr = sch.op("tensor", lambda e, c=c: e.transpose(
                            out=PT[:, c * 128:(c + 1) * 128],
                            in_=qtm[:, c * 128:(c + 1) * 128],
                            identity=identb[:]),
                            deps=([ev, psfree["PT"]] if c == 0 else ()),
                            inc=(c == NCH - 1))
                    dst3 = dst.rearrange("p (c t) -> p c t", t=L)
                    pte = sch.op("scalar", lambda e, dst3=dst3, tt=tt: e.copy(
                        out=dst3[:, :, tt * 128:tt * 128 + 128],
                        in_=PT[:, :].rearrange("p (c t) -> p c t", t=128)),
                        deps=[ttr])
                    psfree["PT"] = pte
                    qtm_free = pte

            wv_ok = load_w(d_wvT, wpool[0], dep=last_qmm)
            all_tr = qtm_free
            rope_units = [(buf, c, g) for buf in (q_fm, k_fm)
                          for c in range(NCH) for g in range(4)]
            pd_free = None
            rope_done = None
            vevac = None

            def emit_rope(buf, c, g):
                nonlocal pd_free, rope_done
                sl = slice(c * L + g * 512, c * L + (g + 1) * 512)
                pmm = sch.op("tensor", lambda e, buf=buf, sl=sl: e.matmul(
                    out=PD[:], lhsT=permb[:], rhs=buf[:, sl],
                    start=True, stop=True), deps=[all_tr, pd_free])
                v1 = sch.op("vector", lambda e, buf=buf, sl=sl, g=g: e.tensor_tensor(
                    out=ropescr[:, 0:512], in0=buf[:, sl],
                    in1=cfm_t[:, g * 512:(g + 1) * 512], op=ALU.mult),
                    deps=[all_tr, t_cf])
                v2 = sch.op("vector", lambda e, g=g: e.tensor_tensor(
                    out=ropescr[:, 512:1024], in0=PD[:],
                    in1=sfm_t[:, g * 512:(g + 1) * 512], op=ALU.mult),
                    deps=[pmm, t_sf])
                rope_done = sch.op("vector", lambda e, buf=buf, sl=sl: e.tensor_tensor(
                    out=buf[:, sl], in0=ropescr[:, 0:512],
                    in1=ropescr[:, 512:1024], op=ALU.add),
                    deps=[v1, v2])
                pd_free = v2

            for tt in range(NT):
                mmtok = proj_tile(PC, "PC", 0, wv_ok, 2, tt)
                vview = va3[:, tt]
                vevac = sch.op("scalar", lambda e, vview=vview: e.copy(
                    out=vview[:, :, 0:64],
                    in_=PC[:, :].rearrange("p (h c) -> p h c", c=64)),
                    deps=[mmtok, va_ones])
                ps_evac["PC"] = vevac
                for u in range(4):
                    buf, c, g = rope_units[4 * tt + u]
                    emit_rope(buf, c, g)

            if s == 0 and "qfm" in dbg:
                out_toks.append(sch.dma("gpsimd", dbg["qfm"][:], q_fm, deps=[rope_done]))
            if s == 0 and "kfm" in dbg:
                out_toks.append(sch.dma("gpsimd", dbg["kfm"][:], k_fm, deps=[rope_done]))
            if s == 0 and "vaug" in dbg:
                out_toks.append(sch.dma("gpsimd", dbg["vaug"][:], v_aug, deps=[vevac]))

            ar.reset(m_A)
            sch.barrier()

            m_B = ar.mark()
            wo_sb = ar.bf16(128, NCH * DM)
            pT = ar.bf16(128, NT * 1024)
            ao_fm = ar.bf16(128, NCH * 512)
            rb_sb = ar.bf16(64, 2 * 512)
            rrow = ar.bf16(1, 2 * 512)
            z1 = ar.f32(128, 1024)
            x_t = ar.f32(128, 2 * 1024)
            x1o = ar.f32(128, 2 * 1024)
            x1hi_t = ar.bf16(128, 1024)
            x1lo_t = ar.bf16(128, 1024)
            tstage = ar.bf16(128, 2 * 1024)

            wo_ok = load_w(d_woT, wo_sb)

            xdma = [None] * NT
            x_t_free = [None, None]
            xdma[0] = sch.dma("gpsimd", x_t[:, 0:1024], d_xtm[s, 0:128, :])

            R = lambda i: rsc[:, i * NE:(i + 1) * NE]
            (lg, e1b, dnm, inh, tC, dist, m1b, zone, wv_, fbb, selb, lg2) = (
                R(0), R(1), R(2), R(3), R(4), R(5), R(6), R(7), R(8), R(9),
                R(10), R(11))
            w_done = None
            tstage_rfree = None
            pending_norm = []

            def flush_norm():
                nonlocal pd_freeB, aoev_last
                while pending_norm:
                    p0, h, rc, stop_tok = pending_norm.pop(0)
                    rb = sch.op("tensor", lambda e, h=h: e.matmul(
                        out=PD[0:64, :], lhsT=ones1[0:1, 0:64],
                        rhs=rrow[0:1, h * 512:(h + 1) * 512],
                        start=True, stop=True), deps=[rc, pd_freeB])
                    rrow_free[h] = rb
                    rbe = sch.op("vector", lambda e, h=h: e.tensor_copy(
                        out=rb_sb[0:64, h * 512:(h + 1) * 512],
                        in_=PD[0:64, :]), deps=[rb, rbsb_free[h]])
                    pd_freeB = rbe
                    ao = sch.op("vector", lambda e, h=h, p0=p0: e.tensor_tensor(
                        out=ao_fm[h * 64:(h + 1) * 64, p0 * 512:(p0 + 1) * 512],
                        in0=PC[0:64, h * 512:(h + 1) * 512],
                        in1=rb_sb[0:64, h * 512:(h + 1) * 512],
                        op=ALU.mult), deps=[rbe, stop_tok, aofm_free])
                    pc_free[h] = ao
                    rbsb_free[h] = ao
                    aoev_last = ao
            av_tok = [None] * 4
            lastS = {"PA": None, "PB": None}
            pc_free = [None, None]
            pd_freeB = None
            rrow_free = [None, None]
            rbsb_free = [None, None]
            aofm_free = None
            aoev_last = None
            x1o_free = [None, None]
            tstage_free = [None, None]
            hilo_free = [None, None]
            z1_free = None

            for slb in range(NSL):
                q0 = slb * 512
                pre_mm = {}

                def emit_smm(p2, kt, pfree):
                    P = PA if kt % 2 == 0 else PB
                    mmtok = None
                    for h in range(2):
                        b0 = h * 64
                        mmtok = sch.op("tensor", lambda e, P=P, p2=p2, h=h, kt=kt, q0=q0, b0=b0: e.matmul(
                            out=P[:, h * 512:(h + 1) * 512],
                            lhsT=k_fm[b0:b0 + 64, p2 * L + kt * 128: p2 * L + kt * 128 + 128],
                            rhs=q_fm[b0:b0 + 64, p2 * L + q0: p2 * L + q0 + 512],
                            start=True, stop=True),
                            deps=([pfree, rope_done] if h == 0 else ()),
                            inc=(h == 1))
                    return mmtok

                for p in range(NH // 2):
                    exps = [None] * NT
                    av_new = [None] * 4
                    last_av = None

                    def emit_av(kc, p=p):
                        nonlocal last_av
                        for h in range(2):
                            hg = 2 * p + h
                            t = sch.op("tensor", lambda e, h=h, hg=hg, kc=kc: e.matmul(
                                out=PC[0:65, h * 512:(h + 1) * 512],
                                lhsT=v_aug[:, kc * NH * 65 + hg * 65: kc * NH * 65 + hg * 65 + 65],
                                rhs=pT[:, kc * 1024 + h * 512: kc * 1024 + (h + 1) * 512],
                                start=(kc == 0), stop=(kc == NT - 1)),
                                deps=([exps[kc], pc_free[h], vevac]
                                      if kc == 0
                                      else ([exps[kc]] if h == 0 else ())),
                                inc=(kc % 4 == 3 and h == 1))
                            if kc % 4 == 3 and h == 1:
                                av_new[kc // 4] = t
                                last_av = t

                    for kt in range(NT):
                        if kt == 4:
                            flush_norm()
                        pname = "PA" if kt % 2 == 0 else "PB"
                        P = PA if kt % 2 == 0 else PB
                        if kt in pre_mm:
                            mmtok = pre_mm.pop(kt)
                        else:
                            pfree = exps[kt - 2] if kt >= 2 else lastS[pname]
                            mmtok = emit_smm(p, kt, pfree)
                        exps[kt] = sch.op("scalar", lambda e, P=P, kt=kt: e.activation(
                            out=pT[:, kt * 1024:(kt + 1) * 1024], in_=P[:],
                            func=AF.Exp, scale=0.125),
                            deps=[mmtok, av_tok[min(3, kt // 4)]])
                        if kt >= 6:
                            emit_av(kt - 6)
                    lastS["PA"] = exps[NT - 2]
                    lastS["PB"] = exps[NT - 1]
                    for kc in range(10, NT):
                        emit_av(kc)
                        if kc >= 14 and p < NH // 2 - 1:
                            kt0 = kc - 14
                            pre_mm[kt0] = emit_smm(p + 1, kt0, exps[14 + kt0])
                    av_tok = av_new
                    for h in range(2):
                        def _rc(e, h=h):
                            with nc.allow_low_precision("softmax 1/sum feeds a bf16 multiply"):
                                return e.reciprocal(
                                    out=rrow[0:1, h * 512:(h + 1) * 512],
                                    in_=PC[64:65, h * 512:(h + 1) * 512])
                        rc = sch.op("vector", _rc, deps=[last_av, rrow_free[h]])
                        pending_norm.append((p, h, rc, last_av))

                if s == 0 and slb == 0 and "pt" in dbg:
                    out_toks.append(sch.dma("gpsimd", dbg["pt"][:], pT, deps=[aoev_last]))
                if s == 0 and slb == 0 and "ao" in dbg:
                    out_toks.append(sch.dma("gpsimd", dbg["ao"][:], ao_fm, deps=[aoev_last]))

                flush_norm()
                oproj_last = None
                for j in range(4):
                    tt = slb * 4 + j
                    P = PA if tt % 2 == 0 else PB
                    pname = "PA" if tt % 2 == 0 else "PB"
                    if tt + 1 < NT:
                        nb = (tt + 1) % 2
                        xdma[tt + 1] = sch.dma(
                            "gpsimd", x_t[:, nb * 1024:(nb + 1) * 1024],
                            d_xtm[s, (tt + 1) * 128:(tt + 2) * 128, :],
                            deps=[x_t_free[nb]])
                    deps0 = [aoev_last, wo_ok, lastS[pname]]
                    for c in range(NCH):
                        for half in range(2):
                            sch.op("tensor", lambda e, P=P, c=c, half=half, j=j: e.matmul(
                                out=P[:, half * 512:(half + 1) * 512],
                                lhsT=ao_fm[:, c * 512 + j * 128: c * 512 + j * 128 + 128],
                                rhs=wo_sb[:, c * DM + half * 512: c * DM + (half + 1) * 512],
                                start=(c == 0), stop=False),
                                deps=(deps0 if c == 0 and half == 0 else ()),
                                inc=False)
                    mmtok = None
                    for half in range(2):
                        mmtok = sch.op("tensor", lambda e, P=P, half=half: e.matmul(
                            out=P[:, half * 512:(half + 1) * 512],
                            lhsT=ones1[:],
                            rhs=browb[0:1, 3 * DM + half * 512: 3 * DM + (half + 1) * 512],
                            start=False, stop=True), inc=(half == 1))
                    if j == 3:
                        oproj_last = mmtok
                    xb = tt % 2
                    xsl = slice(xb * 1024, (xb + 1) * 1024)
                    v1 = sch.op("vector", lambda e, P=P, xsl=xsl: e.tensor_tensor(
                        out=z1[:], in0=P[:], in1=x_t[:, xsl], op=ALU.add),
                        deps=[mmtok, xdma[tt], z1_free])
                    lastS[pname] = v1
                    v2 = sch.op("vector", lambda e: e.reduce_sum(
                        out=stats[:, 0:1], in_=z1[:], axis=AX.X), deps=[v1])
                    s1 = sch.op("scalar", lambda e, xsl=xsl: e.activation(
                        out=x_t[:, xsl], in_=z1[:], func=AF.Square,
                        accum_out=stats[:, 1:2]), deps=[v1])
                    x_t_free[xb] = s1
                    v3 = sch.op("vector", lambda e: e.tensor_scalar(
                        out=stats[:, 2:3], in0=stats[:, 0:1], scalar1=-1.0 / DM,
                        scalar2=None, op0=ALU.mult), deps=[v2])
                    v4 = sch.op("vector", lambda e: e.tensor_tensor(
                        out=stats[:, 3:4], in0=stats[:, 2:3], in1=stats[:, 2:3],
                        op=ALU.mult), deps=[v3])
                    v5 = sch.op("vector", lambda e: e.tensor_scalar(
                        out=stats[:, 4:5], in0=stats[:, 1:2], scalar1=1.0 / DM,
                        scalar2=None, op0=ALU.mult), deps=[s1])
                    v6 = sch.op("vector", lambda e: e.tensor_tensor(
                        out=stats[:, 5:6], in0=stats[:, 4:5], in1=stats[:, 3:4],
                        op=ALU.subtract), deps=[v4, v5])
                    s2 = sch.op("scalar", lambda e: e.activation(
                        out=stats[:, 6:7], in_=stats[:, 5:6], func=AF.Ln,
                        bias=epscol[:]), deps=[v6])
                    s3 = sch.op("scalar", lambda e: e.activation(
                        out=stats[:, 7:8], in_=stats[:, 6:7], func=AF.Exp,
                        scale=-0.5), deps=[s2])
                    ob = tt % 2
                    osl = slice(ob * 1024, (ob + 1) * 1024)
                    v7 = sch.op("vector", lambda e, osl=osl: e.tensor_scalar(
                        out=x1o[:, osl], in0=z1[:], scalar1=stats[:, 2:3],
                        scalar2=stats[:, 7:8], op0=ALU.add, op1=ALU.mult),
                        deps=[v1, v3, s3, x1o_free[ob]])
                    s4 = sch.op("scalar", lambda e, osl=osl: e.copy(
                        out=x1hi_t[:], in_=x1o[:, osl]), deps=[v7, hilo_free[0]])
                    v8 = sch.op("vector", lambda e, osl=osl: e.tensor_tensor(
                        out=x1lo_t[:], in0=x1o[:, osl], in1=x1hi_t[:],
                        op=ALU.subtract), deps=[v7, s4, hilo_free[1]])
                    z1_free = v7
                    ptes = []
                    for which, (buf, tok_in) in enumerate(
                            ((x1hi_t, s4), (x1lo_t, v8))):
                        hsl = slice(which * 1024, (which + 1) * 1024)
                        ttr = None
                        for c in range(NCH):
                            ttr = sch.op("tensor", lambda e, buf=buf, c=c: e.transpose(
                                out=PT[:, c * 128:(c + 1) * 128],
                                in_=buf[:, c * 128:(c + 1) * 128],
                                identity=identb[:]),
                                deps=([tok_in, psfree["PT"]] if c == 0 else ()),
                                inc=(c == NCH - 1))
                        hilo_free[which] = ttr
                        pte = sch.op("scalar", lambda e, hsl=hsl: e.copy(
                            out=tstage[:, hsl], in_=PT[:]),
                            deps=[ttr, tstage_free[which], tstage_rfree])
                        psfree["PT"] = pte
                        ptes.append(pte)
                    dview = d_x1hi.rearrange("(c p) t -> p c t", p=128)
                    tdo = sch.dma("gpsimd",
                                  dview[:, :, tt * 128:(tt + 1) * 128],
                                  tstage[:, 0:1024].rearrange("p (c t) -> p c t", t=128),
                                  deps=[ptes[0]])
                    tstage_free[0] = tdo
                    tdx = sch.dma("gpsimd", d_x1tm[tt * 128:(tt + 1) * 128, :],
                                  x1o[:, osl], deps=[v7, s4, v8])
                    x1o_free[ob] = tdx
                    for c in range(NCH):
                        sch.op("tensor", lambda e, c=c: e.matmul(
                            out=PD[:, 0:16],
                            lhsT=tstage[:, c * 128:(c + 1) * 128],
                            rhs=whilo[:, c * 16:(c + 1) * 16],
                            start=(c == 0), stop=(c == NCH - 1)),
                            deps=([ptes[0], ptes[1], pd_freeB] if c == 0 else ()),
                            inc=False)
                    rmm = None
                    for c in range(NCH):
                        rmm = sch.op("tensor", lambda e, c=c: e.matmul(
                            out=PD[:, 16:24],
                            lhsT=tstage[:, 1024 + c * 128: 1024 + (c + 1) * 128],
                            rhs=whilo[:, c * 16: c * 16 + 8],
                            start=(c == 0), stop=(c == NCH - 1)),
                            inc=(c == NCH - 1))
                    tstage_rfree = rmm
                    sch.op("vector", lambda e: e.tensor_copy(
                        out=lg2[:], in_=PD[:, 0:8]), deps=[rmm])
                    sch.op("vector", lambda e: e.tensor_tensor(
                        out=lg2[:], in0=lg2[:], in1=PD[:, 8:16], op=ALU.add))
                    pd_freeB = sch.op("vector", lambda e: e.tensor_tensor(
                        out=lg[:], in0=lg2[:], in1=PD[:, 16:24], op=ALU.add))
                    sch.op("vector", lambda e: e.tensor_scalar(
                        out=dnm[:], in0=lg[:], scalar1=invT[:, 0:1], scalar2=-1.0,
                        op0=ALU.mult, op1=ALU.mult))
                    t_e1 = sch.op("scalar", lambda e: e.activation(
                        out=e1b[:], in_=dnm[:], func=AF.Exp),
                        deps=[sch.lasttok["vector"]])
                    sch.op("vector", lambda e: e.tensor_scalar(
                        out=dnm[:], in0=e1b[:], scalar1=1.0, scalar2=None,
                        op0=ALU.add), deps=[t_e1])
                    sch.op("vector", lambda e: e.reciprocal(out=inh[:], in_=dnm[:]))
                    sch.op("vector", lambda e: e.tensor_scalar(
                        out=tC[:], in0=inh[:], scalar1=-GOLD_C, scalar2=None,
                        op0=ALU.add))
                    t_ab = sch.op("scalar", lambda e: e.activation(
                        out=dist[:], in_=tC[:], func=AF.Abs),
                        deps=[sch.lasttok["vector"]])
                    sch.op("vector", lambda e: e.tensor_scalar(
                        out=m1b[:], in0=inh[:], scalar1=GOLD_L, scalar2=None,
                        op0=ALU.is_ge))
                    sch.op("vector", lambda e: e.tensor_scalar(
                        out=zone[:], in0=inh[:], scalar1=GOLD_U, scalar2=None,
                        op0=ALU.is_le))
                    sch.op("vector", lambda e: e.tensor_tensor(
                        out=zone[:], in0=zone[:], in1=m1b[:], op=ALU.mult))
                    t_ew = sch.op("scalar", lambda e: e.activation(
                        out=wv_[:], in_=dist[:], func=AF.Exp, scale=-10.0),
                        deps=[t_ab, sch.lasttok["vector"]])
                    t_fb = sch.op("scalar", lambda e: e.activation(
                        out=fbb[:], in_=dist[:], func=AF.Exp, scale=-10.0 / 3.0),
                        deps=[t_ew])
                    sch.op("vector", lambda e: e.tensor_tensor(
                        out=wv_[:], in0=wv_[:], in1=zone[:], op=ALU.mult),
                        deps=[t_ew])
                    sch.op("vector", lambda e: e.reduce_sum(
                        out=stats[:, 8:9], in_=wv_[:], axis=AX.X))
                    sch.op("vector", lambda e: e.tensor_scalar(
                        out=stats[:, 9:10], in0=stats[:, 8:9], scalar1=1e-8,
                        scalar2=None, op0=ALU.is_lt))
                    sch.op("vector", lambda e: e.reduce_max(
                        out=stats[:, 10:11], in_=fbb[:], axis=AX.X), deps=[t_fb])
                    sch.op("vector", lambda e: e.tensor_scalar(
                        out=m1b[:], in0=fbb[:], scalar1=stats[:, 10:11],
                        scalar2=None, op0=ALU.is_ge))
                    sch.op("vector", lambda e: e.tensor_scalar(
                        out=selb[:], in0=m1b[:], scalar1=-1.0, scalar2=1.0,
                        op0=ALU.mult, op1=ALU.add))
                    sch.op("vector", lambda e: e.tensor_tensor(
                        out=selb[:], in0=fbb[:], in1=selb[:], op=ALU.mult))
                    sch.op("vector", lambda e: e.reduce_max(
                        out=stats[:, 11:12], in_=selb[:], axis=AX.X))
                    sch.op("vector", lambda e: e.tensor_scalar(
                        out=selb[:], in0=selb[:], scalar1=stats[:, 11:12],
                        scalar2=None, op0=ALU.is_ge))
                    sch.op("vector", lambda e: e.tensor_tensor(
                        out=m1b[:], in0=m1b[:], in1=selb[:], op=ALU.add))
                    sch.op("vector", lambda e: e.tensor_tensor(
                        out=fbb[:], in0=fbb[:], in1=m1b[:], op=ALU.mult))
                    sch.op("vector", lambda e: e.reduce_sum(
                        out=stats[:, 12:13], in_=fbb[:], axis=AX.X))
                    sch.op("vector", lambda e: e.tensor_scalar_max(
                        out=stats[:, 12:13], in0=stats[:, 12:13], scalar1=1e-8))
                    sch.op("vector", lambda e: e.reciprocal(
                        out=stats[:, 13:14], in_=stats[:, 12:13]))
                    sch.op("vector", lambda e: e.tensor_scalar(
                        out=stats[:, 14:15], in0=stats[:, 9:10], scalar1=-1.0,
                        scalar2=1.0, op0=ALU.mult, op1=ALU.add))
                    sch.op("vector", lambda e: e.tensor_scalar(
                        out=wv_[:], in0=wv_[:], scalar1=stats[:, 14:15],
                        scalar2=None, op0=ALU.mult))
                    sch.op("vector", lambda e: e.tensor_scalar(
                        out=fbb[:], in0=fbb[:], scalar1=stats[:, 13:14],
                        scalar2=None, op0=ALU.mult))
                    sch.op("vector", lambda e: e.tensor_scalar(
                        out=fbb[:], in0=fbb[:], scalar1=stats[:, 9:10],
                        scalar2=None, op0=ALU.mult))
                    sch.op("vector", lambda e: e.tensor_tensor(
                        out=wv_[:], in0=wv_[:], in1=fbb[:], op=ALU.add))
                    sch.op("vector", lambda e: e.reduce_sum(
                        out=stats[:, 15:16], in_=wv_[:], axis=AX.X))
                    sch.op("vector", lambda e: e.tensor_scalar_max(
                        out=stats[:, 15:16], in0=stats[:, 15:16], scalar1=1e-8))
                    sch.op("vector", lambda e: e.reciprocal(
                        out=stats[:, 8:9], in_=stats[:, 15:16]))
                    w_done = sch.op("vector", lambda e, tt=tt: e.tensor_scalar(
                        out=w_tm[:, tt * NE:(tt + 1) * NE], in0=wv_[:],
                        scalar1=stats[:, 8:9], scalar2=None, op0=ALU.mult))
                aofm_free = oproj_last

            ar.reset(m_span)
            sch.barrier()

            acc = ar.bf16(128, 2 * 8 * 1024)
            x1t_in = ar.f32(128, 2 * 1024)
            outst = ar.f32(128, 2 * 1024)
            z2 = ar.f32(128, 2 * 1024)
            x1hi_fm = ar.bf16(128, NCH * L)
            t_hi = sch.dma("sync", x1hi_fm.rearrange("p (c t) -> p c t", t=L),
                           d_x1hi.rearrange("(c p) t -> p c t", p=128))
            if s == 0 and "x1hi" in dbg:
                out_toks.append(sch.dma("gpsimd", dbg["x1hi"][:],
                                        d_x1hi[:], deps=[t_hi]))
            if s == 0 and "x1tmv" in dbg:
                out_toks.append(sch.dma("gpsimd", dbg["x1tmv"][:], d_x1tm[:]))
            if s == 0 and "w_tm" in dbg:
                out_toks.append(sch.dma("gpsimd", dbg["w_tm"][:], w_tm[:],
                                        deps=[w_done]))

            w_fm = ar.bf16(NE, NT * 128)
            wpoolF = [ar.bf16(128, NCH * DM) for _ in range(4)]
            hw = ar.bf16(128, NCH * 512)
            sg = ar.f32(128, 512)
            h_sb = ar.bf16(128, 512)
            wb_sb = ar.bf16(128, 512)

            wfm_ev = None
            for tt in range(NT):
                wtr = sch.op("tensor", lambda e, tt=tt: e.transpose(
                    out=PT[0:NE, 0:128], in_=w_tm[:, tt * NE:(tt + 1) * NE],
                    identity=identb[:]),
                    deps=[w_done, psfree["PT"]])
                wfm_ev = sch.op("vector", lambda e, tt=tt: e.tensor_copy(
                    out=w_fm[0:NE, tt * 128:(tt + 1) * 128],
                    in_=PT[0:NE, 0:128]), deps=[wtr])
                psfree["PT"] = wfm_ev
            wfm_ok = wfm_ev

            wslot_free = [None] * 4
            slotctr = [0]

            def load_wffn(dram2d):
                slot = slotctr[0] % 4
                slotctr[0] += 1
                tok = sch.dma("gpsimd",
                              wpoolF[slot].rearrange("p (c i) -> p c i", i=DM),
                              dram2d.rearrange("(c p) i -> p c i", p=128),
                              deps=[wslot_free[slot]])
                return slot, tok

            gu_free = [None, None]
            pc_freeF = [None, None]
            pd_wb = pd_freeB
            hw_free = None
            wb_free = None
            sg_free = None
            hsb_free = None
            acc_free = [None] * 8
            acc_tok = [None] * 8
            x1t_free = [None, None]
            out_free = [None, None]
            preW = {}

            def get_w(wp2, e2):
                if (wp2, e2) in preW:
                    return preW.pop((wp2, e2))
                return (load_wffn(d_wg[e2]), load_wffn(d_wu[e2]),
                        load_wffn(d_wd[e2]))

            for wpass in range(2):
                for e_i in range(NE):
                    (slot_g, tg), (slot_u, tu), (slot_d, td_) = get_w(wpass, e_i)
                    for sli in range(2):
                        slb2 = 2 * wpass + sli
                        q0 = slb2 * 512
                        wbmm = sch.op("tensor", lambda e, e_i=e_i, slb2=slb2: e.matmul(
                            out=PD[:], lhsT=onehb[:, e_i * 128:(e_i + 1) * 128],
                            rhs=w_fm[0:NE, slb2 * 512:(slb2 + 1) * 512],
                            start=True, stop=True), deps=[wfm_ok, pd_wb])
                        wbe = sch.op("scalar", lambda e: e.copy(
                            out=wb_sb[:], in_=PD[:]), deps=[wbmm, wb_free])
                        pd_wb = wbe
                        hw_last = None
                        for it in range(NCH):
                            gdep = [tg, gu_free[0], t_hi]
                            gtok = None
                            for c in range(NCH):
                                gtok = sch.op("tensor", lambda e, c=c, it=it, q0=q0, slot_g=slot_g: e.matmul(
                                    out=PA[:, 0:512],
                                    lhsT=wpoolF[slot_g][:, c * DM + it * 128: c * DM + it * 128 + 128],
                                    rhs=x1hi_fm[:, c * L + q0: c * L + q0 + 512],
                                    start=(c == 0), stop=(c == NCH - 1)),
                                    deps=(gdep if c == 0 else ()),
                                    inc=(c == NCH - 1))
                            udep = [tu, gu_free[1]]
                            utok = None
                            for c in range(NCH):
                                utok = sch.op("tensor", lambda e, c=c, it=it, q0=q0, slot_u=slot_u: e.matmul(
                                    out=PA[:, 512:1024],
                                    lhsT=wpoolF[slot_u][:, c * DM + it * 128: c * DM + it * 128 + 128],
                                    rhs=x1hi_fm[:, c * L + q0: c * L + q0 + 512],
                                    start=(c == 0), stop=(c == NCH - 1)),
                                    deps=(udep if c == 0 else ()),
                                    inc=(c == NCH - 1))
                            sgt = sch.op("scalar", lambda e: e.activation(
                                out=sg[:], in_=PA[:, 0:512], func=AF.Silu),
                                deps=[gtok, sg_free])
                            gu_free[0] = sgt
                            ht = sch.op("vector", lambda e: e.tensor_tensor(
                                out=h_sb[:], in0=sg[:], in1=PA[:, 512:1024],
                                op=ALU.mult), deps=[sgt, utok, hsb_free])
                            sg_free = ht
                            gu_free[1] = ht
                            hwt = sch.op("vector", lambda e, it=it: e.tensor_tensor(
                                out=hw[:, it * 512:(it + 1) * 512], in0=h_sb[:],
                                in1=wb_sb[:], op=ALU.mult),
                                deps=[ht, wbe, hw_free])
                            hsb_free = hwt
                            hw_last = hwt
                        wb_free = hw_last
                        for jj in range(4):
                            ttp = sli * 4 + jj
                            Pd2 = PC if jj % 2 == 0 else PB
                            dep0 = [td_, hw_last, pc_freeF[jj % 2]]
                            dtok = None
                            for ic in range(NCH):
                                for half in range(2):
                                    dtok = sch.op("tensor", lambda e, Pd2=Pd2, ic=ic, half=half, jj=jj, slot_d=slot_d: e.matmul(
                                        out=Pd2[:, half * 512:(half + 1) * 512],
                                        lhsT=hw[:, ic * 512 + jj * 128: ic * 512 + jj * 128 + 128],
                                        rhs=wpoolF[slot_d][:, ic * DM + half * 512: ic * DM + (half + 1) * 512],
                                        start=(ic == 0), stop=(ic == NCH - 1)),
                                        deps=(dep0 if ic == 0 and half == 0 else ()),
                                        inc=(ic == NCH - 1 and half == 1))
                            ao2 = (wpass % 2) * 8192 + ttp * 1024
                            if e_i == 0:
                                def _ev0(e, ao2=ao2, Pd2=Pd2):
                                    with nc.allow_low_precision("expert partial sums kept in bf16"):
                                        return e.tensor_copy(
                                            out=acc[:, ao2:ao2 + 1024],
                                            in_=Pd2[:])
                                ev = sch.op("vector", _ev0, deps=[dtok])
                            else:
                                def _eva(e, ao2=ao2, Pd2=Pd2):
                                    with nc.allow_low_precision("expert partial sums kept in bf16"):
                                        return e.tensor_tensor(
                                            out=acc[:, ao2:ao2 + 1024],
                                            in0=acc[:, ao2:ao2 + 1024],
                                            in1=Pd2[:], op=ALU.add)
                                ev = sch.op("vector", _eva, deps=[dtok])
                            acc_tok[ttp] = ev
                            pc_freeF[jj % 2] = ev
                        hw_free = dtok
                    wslot_free[slot_g] = sch.ttok()
                    wslot_free[slot_u] = sch.ttok()
                    wslot_free[slot_d] = sch.ttok()

                if wpass == 0:
                    preW[(1, 0)] = (load_wffn(d_wg[0]), load_wffn(d_wu[0]),
                                    load_wffn(d_wd[0]))
                if wpass == 1:
                    ffn_core = [sch.ttok(), sch.lasttok["vector"],
                                sch.lasttok["scalar"]]
                for sli in range(2):
                    slb2 = 2 * wpass + sli
                    for jj0 in (0, 2):
                        pair = []
                        for jj in (jj0, jj0 + 1):
                            tt = slb2 * 4 + jj
                            pair.append(dict(
                                tt=tt, ttp=sli * 4 + jj, xb=tt % 2,
                                bk=8 * (jj % 2),
                                xsl=slice((tt % 2) * 1024, (tt % 2 + 1) * 1024),
                                zsl=slice((jj % 2) * 1024, (jj % 2 + 1) * 1024),
                                asl=slice((wpass % 2) * 8192 + (sli * 4 + jj) * 1024,
                                          (wpass % 2) * 8192 + (sli * 4 + jj + 1) * 1024)))
                        for ti in pair:
                            ti["txd"] = sch.dma(
                                "gpsimd", x1t_in[:, ti["xsl"]],
                                d_x1tm[ti["tt"] * 128:(ti["tt"] + 1) * 128, :],
                                deps=[x1t_free[ti["xb"]]])
                        for ti in pair:
                            ti["v1"] = sch.op("vector", lambda e, ti=ti: e.tensor_tensor(
                                out=z2[:, ti["zsl"]], in0=acc[:, ti["asl"]],
                                in1=x1t_in[:, ti["xsl"]], op=ALU.add),
                                deps=[acc_tok[ti["ttp"]], ti["txd"]])
                            x1t_free[ti["xb"]] = ti["v1"]
                        for ti in pair:
                            bk = ti["bk"]
                            ti["v2"] = sch.op("vector", lambda e, ti=ti, bk=bk: e.reduce_sum(
                                out=stats[:, bk:bk + 1], in_=z2[:, ti["zsl"]],
                                axis=AX.X), deps=[ti["v1"]])
                            ti["s1"] = sch.op("scalar", lambda e, ti=ti, bk=bk: e.activation(
                                out=acc[:, ti["asl"]], in_=z2[:, ti["zsl"]],
                                func=AF.Square,
                                accum_out=stats[:, bk + 1:bk + 2]), deps=[ti["v1"]])
                            acc_free[ti["ttp"]] = ti["s1"]
                        for ti in pair:
                            bk = ti["bk"]
                            ti["v3"] = sch.op("vector", lambda e, bk=bk: e.tensor_scalar(
                                out=stats[:, bk + 2:bk + 3], in0=stats[:, bk:bk + 1],
                                scalar1=-1.0 / DM, scalar2=None, op0=ALU.mult),
                                deps=[ti["v2"]])
                        for ti in pair:
                            bk = ti["bk"]
                            ti["v4"] = sch.op("vector", lambda e, bk=bk: e.tensor_tensor(
                                out=stats[:, bk + 3:bk + 4], in0=stats[:, bk + 2:bk + 3],
                                in1=stats[:, bk + 2:bk + 3], op=ALU.mult),
                                deps=[ti["v3"]])
                        for ti in pair:
                            bk = ti["bk"]
                            ti["v5"] = sch.op("vector", lambda e, bk=bk: e.tensor_scalar(
                                out=stats[:, bk + 4:bk + 5], in0=stats[:, bk + 1:bk + 2],
                                scalar1=1.0 / DM, scalar2=None, op0=ALU.mult),
                                deps=[ti["s1"]])
                        for ti in pair:
                            bk = ti["bk"]
                            ti["v6"] = sch.op("vector", lambda e, bk=bk: e.tensor_tensor(
                                out=stats[:, bk + 5:bk + 6], in0=stats[:, bk + 4:bk + 5],
                                in1=stats[:, bk + 3:bk + 4], op=ALU.subtract),
                                deps=[ti["v4"], ti["v5"]])
                        for ti in pair:
                            bk = ti["bk"]
                            ti["s2"] = sch.op("scalar", lambda e, bk=bk: e.activation(
                                out=stats[:, bk + 6:bk + 7], in_=stats[:, bk + 5:bk + 6],
                                func=AF.Ln, bias=epscol[:]), deps=[ti["v6"]])
                        for ti in pair:
                            bk = ti["bk"]
                            ti["s3"] = sch.op("scalar", lambda e, bk=bk: e.activation(
                                out=stats[:, bk + 7:bk + 8], in_=stats[:, bk + 6:bk + 7],
                                func=AF.Exp, scale=-0.5), deps=[ti["s2"]])
                        for ti in pair:
                            bk = ti["bk"]
                            ob = ti["tt"] % 2
                            osl2 = slice(ob * 1024, (ob + 1) * 1024)
                            v7 = sch.op("vector", lambda e, ti=ti, bk=bk, osl2=osl2:
                                        e.tensor_scalar(
                                out=outst[:, osl2], in0=z2[:, ti["zsl"]],
                                scalar1=stats[:, bk + 2:bk + 3],
                                scalar2=stats[:, bk + 7:bk + 8],
                                op0=ALU.add, op1=ALU.mult),
                                deps=[ti["v1"], ti["v3"], ti["s3"], out_free[ob]])
                            tdo = sch.dma("gpsimd",
                                          d_out[s, ti["tt"] * 128:(ti["tt"] + 1) * 128, :],
                                          outst[:, osl2], deps=[v7])
                            out_free[ob] = tdo
                            out_toks.append(tdo)

            ar.reset(m_span)
            if s + 1 < SEQ:
                f = [(sname, v) for sname, v in sch.cnt.items() if v > 0]
                for e2 in ("tensor", "scalar", "vector"):
                    sch.pending[e2].extend(f)
                for e2 in ("sync", "gpsimd"):
                    sch.pending[e2].extend(ffn_core)
            else:
                sch.barrier()

        sch.wait_raw("gpsimd", [("DG", sch.cnt["DG"])])
        sch.wait_raw("sync", [("DS", sch.cnt["DS"])])
        sch.finalize(block, sems)
    return nc


def _host_consts():
    inv = 1.0 / (10000.0 ** (np.arange(0, HD, 2, dtype=np.float64) / HD))
    t = np.arange(L, dtype=np.float64)[:, None]
    fr = t * inv[None, :]
    cos = np.cos(fr)
    sin = np.sin(fr)
    cfm = np.empty((128, L), np.float64)
    sfm = np.empty((128, L), np.float64)
    for p in range(128):
        i = p % HD
        cfm[p] = cos[:, i % 32]
        sfm[p] = (-1.0 if i < 32 else 1.0) * sin[:, i % 32]
    perm = np.zeros((128, 128), np.float32)
    for m in range(128):
        i = m % HD
        k = m + 32 if i < 32 else m - 32
        perm[k, m] = 1.0
    oneh = np.zeros((NE, NE * 128), np.float32)
    for e in range(NE):
        oneh[e, e * 128:(e + 1) * 128] = 1.0
    return (np.ascontiguousarray(cfm.astype(np.float32)),
            np.ascontiguousarray(sfm.astype(np.float32)),
            perm, np.eye(128, dtype=np.float32), oneh)


_PROG_CACHE = {}


def kernel(**inputs):
    x = np.ascontiguousarray(np.asarray(inputs["L0_attn_x"], dtype=np.float32))
    Wq = np.asarray(inputs["L0_attn_Wq"], np.float32)
    bq = np.asarray(inputs["L0_attn_bq"], np.float32)
    Wk = np.asarray(inputs["L0_attn_Wk"], np.float32)
    bk = np.asarray(inputs["L0_attn_bk"], np.float32)
    Wv = np.asarray(inputs["L0_attn_Wv"], np.float32)
    bv = np.asarray(inputs["L0_attn_bv"], np.float32)
    Wo = np.asarray(inputs["L0_attn_Wo"], np.float32)
    bo = np.asarray(inputs["L0_attn_bo"], np.float32)
    Wg = np.ascontiguousarray(np.asarray(inputs["L0_ffn_Wg"], np.float32))
    Wu = np.ascontiguousarray(np.asarray(inputs["L0_ffn_Wu"], np.float32))
    Wd = np.ascontiguousarray(np.asarray(inputs["L0_ffn_Wd"], np.float32))
    Wr = np.ascontiguousarray(np.asarray(inputs["L0_ffn_Wr"], np.float32))
    temp = np.asarray(inputs["L0_ffn_temperature"], np.float32)

    cfm, sfm, perm, ident, oneh = _host_consts()
    wqT = np.ascontiguousarray(Wq.T)
    wkT = np.ascontiguousarray(Wk.T)
    wvT = np.ascontiguousarray(Wv.T)
    woT = np.ascontiguousarray(Wo.T)
    tbc = np.ascontiguousarray(np.broadcast_to(temp.reshape(1, 1), (128, 1)))

    key = tuple(DEBUG)
    if key not in _PROG_CACHE:
        _PROG_CACHE[key] = build_program()
    nc = _PROG_CACHE[key]

    xs = x.reshape(NCORES, SEQ, L, DM)
    in_maps = []
    for c in range(NCORES):
        xc = np.ascontiguousarray(xs[c])
        xTc = np.ascontiguousarray(xc.transpose(0, 2, 1))
        in_maps.append(dict(
            xtm=xc, xfm=xTc, wqT=wqT, wkT=wkT, wvT=wvT, woT=woT,
            bq=bq.reshape(1, DM), bk=bk.reshape(1, DM),
            bv=bv.reshape(1, DM), bo=bo.reshape(1, DM),
            wg=Wg, wu=Wu, wd=Wd, wr=Wr, tbc=tbc,
            cfm=cfm, sfm=sfm, ident=ident, permm=perm, oneh=oneh))

    res = run_bass_kernel_spmd(nc, in_maps, core_ids=list(range(NCORES)))
    out = np.concatenate(
        [np.asarray(r["out"]).reshape(SEQ, L, DM) for r in res.results], axis=0)
    if DEBUG:
        kernel.dbg = [{k: np.asarray(v) for k, v in r.items()
                       if k.startswith("dbg_")} for r in res.results]
    return out.reshape(16, L, DM)


# version 29
# speedup vs baseline: 1.0481x; 1.0481x over previous
import numpy as np
from contextlib import ExitStack

import concourse.bass as bass
import concourse.mybir as mybir
from concourse.bass_utils import run_bass_kernel_spmd

F32 = mybir.dt.float32
BF16 = mybir.dt.bfloat16
AF = mybir.ActivationFunctionType
ALU = mybir.AluOpType
AX = mybir.AxisListType

NCORES = 8
SEQ = 2
L = 2048
DM = 1024
NH = 16
HD = 64
NE = 8
NT = 16
NSL = 4
NCH = 8

GOLD_C = float(1.0 / np.e)
GOLD_L = float(0.5 - np.log(4.0 / 3.0))
GOLD_U = 0.5

DEBUG = ()

ENGINES = ("sync", "tensor", "scalar", "vector", "gpsimd")
AUTOCHAIN = ("vector",)


class Sched:

    def __init__(self):
        self.prog = {e: [] for e in ENGINES}
        self.cnt = {"T": 0, "S": 0, "V": 0, "G": 0, "DS": 0, "DG": 0}
        self.seen = {e: {} for e in ENGINES}
        self.esem = {"tensor": "T", "scalar": "S", "vector": "V", "gpsimd": "G"}
        self.qsem = {"sync": "DS", "gpsimd": "DG"}
        self.pending = {e: [] for e in ENGINES}
        self.lasttok = {e: None for e in ENGINES}
        self.h = None

    def _waits(self, eng, deps):
        out = []
        seen = self.seen[eng]
        alldeps = list(self.pending[eng]) + [d for d in deps if d is not None]
        self.pending[eng] = []
        for sem, val in alldeps:
            if eng == "tensor" and sem == "T":
                continue
            if seen.get(sem, 0) >= val:
                continue
            seen[sem] = val
            out.append((sem, val))
        return out

    def op(self, eng, fn, deps=(), inc=True):
        deps = list(deps)
        if eng in AUTOCHAIN:
            deps.append(self.lasttok[eng])
            inc = True
        waits = self._waits(eng, deps)
        sem = self.esem[eng]
        tok = None
        if inc:
            self.cnt[sem] += 1
            tok = (sem, self.cnt[sem])
            self.lasttok[eng] = tok

        def emit(e, waits=waits, fn=fn, inc=inc, sem=sem, sched=self):
            for s, val in waits:
                e.wait_ge(sched.h[s], val)
            ins = fn(e)
            if inc:
                ins.then_inc(sched.h[sem], 1)
        self.prog[eng].append(emit)
        return tok

    def dma(self, eng, out, in_, deps=()):
        waits = self._waits(eng, deps)
        q = self.qsem[eng]
        self.cnt[q] += 16
        tok = (q, self.cnt[q])

        def emit(e, waits=waits, out=out, in_=in_, q=q, sched=self):
            for s, val in waits:
                e.wait_ge(sched.h[s], val)
            e.dma_start(out=out, in_=in_).then_inc(sched.h[q], 16)
        self.prog[eng].append(emit)
        return tok

    def ttok(self):
        return ("T", self.cnt["T"])

    def barrier(self):
        f = [(s, v) for s, v in self.cnt.items() if v > 0]
        for e in ENGINES:
            self.pending[e].extend(f)

    def wait_raw(self, eng, deps):
        waits = self._waits(eng, deps)

        def emit(e, waits=waits, sched=self):
            for s, val in waits:
                e.wait_ge(sched.h[s], val)
        self.prog[eng].append(emit)

    def finalize(self, block, handles):
        self.h = handles
        for eng in ENGINES:
            lst = self.prog[eng]
            if not lst:
                continue
            sec = getattr(block, eng)

            def runner(e, lst=lst):
                for f in lst:
                    f(e)
            sec(runner)


class Arena:

    def __init__(self, hb, nb, hf, nf):
        self.hb, self.nb = hb, nb
        self.hf, self.nf = hf, nf
        self.ob = 0
        self.of = 0

    def bf16(self, rows, cols):
        cols_a = (cols + 15) & ~15
        assert self.ob + cols_a <= self.nb, "bf16 arena overflow"
        o = self.ob
        self.ob += cols_a
        return self.hb[0:rows, o:o + cols]

    def f32(self, rows, cols):
        cols_a = (cols + 7) & ~7
        assert self.of + cols_a <= self.nf, "f32 arena overflow"
        o = self.of
        self.of += cols_a
        return self.hf[0:rows, o:o + cols]

    def mark(self):
        return (self.ob, self.of)

    def reset(self, m):
        self.ob, self.of = m


def build_program():
    nc = bass.Bass("TRN2", target_bir_lowering=False)

    d_xtm = nc.dram_tensor("xtm", [SEQ, L, DM], F32, kind="ExternalInput").ap()
    d_xfm = nc.dram_tensor("xfm", [SEQ, DM, L], F32, kind="ExternalInput").ap()
    d_wqT = nc.dram_tensor("wqT", [DM, DM], F32, kind="ExternalInput").ap()
    d_wkT = nc.dram_tensor("wkT", [DM, DM], F32, kind="ExternalInput").ap()
    d_wvT = nc.dram_tensor("wvT", [DM, DM], F32, kind="ExternalInput").ap()
    d_woT = nc.dram_tensor("woT", [DM, DM], F32, kind="ExternalInput").ap()
    d_b = {k: nc.dram_tensor("b" + k, [1, DM], F32, kind="ExternalInput").ap()
           for k in "qkvo"}
    d_wg = nc.dram_tensor("wg", [NE, DM, DM], F32, kind="ExternalInput").ap()
    d_wu = nc.dram_tensor("wu", [NE, DM, DM], F32, kind="ExternalInput").ap()
    d_wd = nc.dram_tensor("wd", [NE, DM, DM], F32, kind="ExternalInput").ap()
    d_wr = nc.dram_tensor("wr", [DM, NE], F32, kind="ExternalInput").ap()
    d_tbc = nc.dram_tensor("tbc", [128, 1], F32, kind="ExternalInput").ap()
    d_cfm = nc.dram_tensor("cfm", [128, L], F32, kind="ExternalInput").ap()
    d_sfm = nc.dram_tensor("sfm", [128, L], F32, kind="ExternalInput").ap()
    d_ident = nc.dram_tensor("ident", [128, 128], F32, kind="ExternalInput").ap()
    d_perm = nc.dram_tensor("permm", [128, 128], F32, kind="ExternalInput").ap()
    d_oneh = nc.dram_tensor("oneh", [NE, NE * 128], F32, kind="ExternalInput").ap()
    d_out = nc.dram_tensor("out", [SEQ, L, DM], F32, kind="ExternalOutput").ap()
    d_x1tm = nc.dram_tensor("x1tm_scr", [L, DM], F32).ap()
    d_x1hi = nc.dram_tensor("x1hi_scr", [DM, L], BF16).ap()
    d_x1lo = nc.dram_tensor("x1lo_scr", [DM, L], BF16).ap()

    dbg = {}
    for name in DEBUG:
        shape, dt = {
            "qfm": ([128, NCH * L], BF16), "kfm": ([128, NCH * L], BF16),
            "vaug": ([128, NT * NH * 65], BF16),
            "ao": ([128, NCH * 512], BF16),
            "pt": ([128, NT * 1024], BF16),
            "w_tm": ([128, NT * NE], BF16),
            "x1hi": ([DM, L], BF16),
            "x1tmv": ([L, DM], F32),
        }[name]
        dbg[name] = nc.dram_tensor("dbg_" + name, shape, dt,
                                   kind="ExternalOutput").ap()

    sch = Sched()
    with ExitStack() as st:
        sems = {}
        for sname in ("T", "S", "V", "G", "DS", "DG"):
            sems[sname] = st.enter_context(nc.semaphore("sem" + sname))

        identb = st.enter_context(nc.sbuf_tensor("identb", [128, 128], BF16))
        permb = st.enter_context(nc.sbuf_tensor("permb", [128, 128], BF16))
        onehb = st.enter_context(nc.sbuf_tensor("onehb", [NE, NE * 128], BF16))
        ones1 = st.enter_context(nc.sbuf_tensor("ones1", [1, 128], BF16))
        browb = st.enter_context(nc.sbuf_tensor("browb", [1, 4 * DM], BF16))
        wrf = st.enter_context(nc.sbuf_tensor("wrf", [128, NCH * NE], F32))
        whilo = st.enter_context(nc.sbuf_tensor("whilo", [128, NCH * 16], BF16))
        tcol = st.enter_context(nc.sbuf_tensor("tcol", [128, 1], F32))
        invT = st.enter_context(nc.sbuf_tensor("invT", [128, 1], F32))
        w_tm = st.enter_context(nc.sbuf_tensor("w_tm", [128, NT * NE], BF16))
        stats = st.enter_context(nc.sbuf_tensor("stats", [128, 16], F32))
        epscol = st.enter_context(nc.sbuf_tensor("epscol", [128, 1], F32))
        rsc = st.enter_context(nc.sbuf_tensor("rsc", [128, 12 * NE], F32))

        ARB_N = 84480
        ARF_N = 6656
        arena_b = st.enter_context(nc.sbuf_tensor("arenab", [128, ARB_N], BF16))
        arena_f = st.enter_context(nc.sbuf_tensor("arenaf", [128, ARF_N], F32))
        ar = Arena(arena_b, ARB_N, arena_f, ARF_N)

        PA = st.enter_context(nc.psum_tensor("PA", [128, 1024], F32))
        PB = st.enter_context(nc.psum_tensor("PB", [128, 1024], F32))
        PC = st.enter_context(nc.psum_tensor("PC", [128, 1024], F32))
        PD = st.enter_context(nc.psum_tensor("PD", [128, 512], F32))
        PT = st.enter_context(nc.psum_tensor("PT", [128, 1024], BF16))

        block = st.enter_context(nc.Block())

        t_consts = []
        t_consts.append(sch.dma("gpsimd", identb[:], d_ident[:]))
        t_consts.append(sch.dma("gpsimd", permb[:], d_perm[:]))
        t_consts.append(sch.dma("gpsimd", onehb[:], d_oneh[:]))
        sch.op("vector", lambda e: e.memset(ones1[:], 1.0))
        sch.op("vector", lambda e: e.memset(epscol[:], 1e-5))
        for i, key in enumerate("qkvo"):
            t_consts.append(sch.dma(
                "gpsimd", browb[0:1, i * DM:(i + 1) * DM], d_b[key][:]))
        td = sch.dma("sync", wrf[:, :].rearrange("p (c e) -> p c e", e=NE),
                     d_wr.rearrange("(c p) e -> p c e", p=128))
        whi3 = whilo[:, :].rearrange("p (c t) -> p c t", t=16)
        wrf3 = wrf[:, :].rearrange("p (c e) -> p c e", e=NE)
        c_whi = sch.op("scalar", lambda e: e.copy(
            out=whi3[:, :, 0:NE], in_=wrf3), deps=[td])
        sch.op("vector", lambda e: e.tensor_tensor(
            out=whi3[:, :, NE:16], in0=wrf3, in1=whi3[:, :, 0:NE],
            op=ALU.subtract), deps=[td, c_whi])
        td = sch.dma("sync", tcol[:], d_tbc[:])
        sch.op("vector", lambda e: e.reciprocal(out=invT[:], in_=tcol[:]),
               deps=[td])
        sch.barrier()

        out_toks = []
        psfree = {"PT": None}

        for s in range(SEQ):
            m_span = ar.mark()
            q_fm = ar.bf16(128, NCH * L)
            k_fm = ar.bf16(128, NCH * L)
            v_aug = ar.bf16(128, NT * NH * 65)

            m_A = ar.mark()
            x_fm = ar.bf16(128, NCH * L)
            wpool = [ar.bf16(128, NCH * DM), ar.bf16(128, NCH * DM)]
            qtm = ar.bf16(128, 1024)
            ropescr = ar.f32(128, 1024)
            cfm_t = ar.f32(128, L)
            sfm_t = ar.f32(128, L)

            frontier_now = [(s2, v2) for s2, v2 in sch.cnt.items() if v2 > 0]
            t_cf = sch.dma("sync", cfm_t, d_cfm[:], deps=frontier_now)
            t_sf = sch.dma("sync", sfm_t, d_sfm[:])

            va3 = v_aug.rearrange("p (b h c) -> p b h c", h=NH, c=65)
            va_ones = sch.op("vector", lambda e: e.memset(va3[:, :, :, 64:65], 1.0))

            xc_last = sch.dma("gpsimd",
                              x_fm.rearrange("p (c t) -> p c t", t=L),
                              d_xfm[s].rearrange("(c p) t -> p c t", p=128))

            def load_w(dram2d, dst, dep=None):
                return sch.dma("gpsimd",
                               dst.rearrange("p (c i) -> p c i", i=DM),
                               dram2d.rearrange("(c p) i -> p c i", p=128),
                               deps=[dep])

            wq_ok = load_w(d_wqT, wpool[0])
            wk_ok = load_w(d_wkT, wpool[1])

            ps_evac = {"PA": None, "PB": None, "PC": None}
            qtm_free = None
            last_qmm = None

            def proj_tile(P, pname, wslot, wtok, bofs, tt, extra=None):
                deps0 = [xc_last, wtok, ps_evac[pname]]
                if extra:
                    deps0 += extra
                for c in range(NCH):
                    for half in range(2):
                        sch.op("tensor", lambda e, P=P, c=c, half=half, tt=tt, wslot=wslot: e.matmul(
                            out=P[:, half * 512:(half + 1) * 512],
                            lhsT=x_fm[:, c * L + tt * 128: c * L + tt * 128 + 128],
                            rhs=wpool[wslot][:, c * DM + half * 512: c * DM + (half + 1) * 512],
                            start=(c == 0), stop=False),
                            deps=(deps0 if (c == 0 and half == 0) else ()), inc=False)
                mmtok = None
                for half in range(2):
                    mmtok = sch.op("tensor", lambda e, P=P, half=half, bofs=bofs: e.matmul(
                        out=P[:, half * 512:(half + 1) * 512],
                        lhsT=ones1[:],
                        rhs=browb[0:1, bofs * DM + half * 512: bofs * DM + (half + 1) * 512],
                        start=False, stop=True), inc=(half == 1))
                return mmtok

            for tt in range(NT):
                for which, (P, pname, wslot, wtok, bofs, dst) in enumerate((
                        (PA, "PA", 0, wq_ok, 0, q_fm),
                        (PB, "PB", 1, wk_ok, 1, k_fm))):
                    mmtok = proj_tile(P, pname, wslot, wtok, bofs, tt)
                    if which == 0 and tt == NT - 1:
                        last_qmm = mmtok
                    ev = sch.op("scalar", lambda e, P=P: e.copy(out=qtm[:], in_=P[:]),
                                deps=[mmtok, qtm_free])
                    ps_evac[pname] = ev
                    ttr = None
                    for c in range(NCH):
                        ttr = sch.op("tensor", lambda e, c=c: e.transpose(
                            out=PT[:, c * 128:(c + 1) * 128],
                            in_=qtm[:, c * 128:(c + 1) * 128],
                            identity=identb[:]),
                            deps=([ev, psfree["PT"]] if c == 0 else ()),
                            inc=(c == NCH - 1))
                    dst3 = dst.rearrange("p (c t) -> p c t", t=L)
                    pte = sch.op("scalar", lambda e, dst3=dst3, tt=tt: e.copy(
                        out=dst3[:, :, tt * 128:tt * 128 + 128],
                        in_=PT[:, :].rearrange("p (c t) -> p c t", t=128)),
                        deps=[ttr])
                    psfree["PT"] = pte
                    qtm_free = pte

            wv_ok = load_w(d_wvT, wpool[0], dep=last_qmm)
            all_tr = qtm_free
            rope_units = [(buf, c, g) for buf in (q_fm, k_fm)
                          for c in range(NCH) for g in range(4)]
            pd_free = None
            rope_done = None
            vevac = None

            def emit_rope(buf, c, g):
                nonlocal pd_free, rope_done
                sl = slice(c * L + g * 512, c * L + (g + 1) * 512)
                pmm = sch.op("tensor", lambda e, buf=buf, sl=sl: e.matmul(
                    out=PD[:], lhsT=permb[:], rhs=buf[:, sl],
                    start=True, stop=True), deps=[all_tr, pd_free])
                v1 = sch.op("vector", lambda e, buf=buf, sl=sl, g=g: e.tensor_tensor(
                    out=ropescr[:, 0:512], in0=buf[:, sl],
                    in1=cfm_t[:, g * 512:(g + 1) * 512], op=ALU.mult),
                    deps=[all_tr, t_cf])
                v2 = sch.op("vector", lambda e, g=g: e.tensor_tensor(
                    out=ropescr[:, 512:1024], in0=PD[:],
                    in1=sfm_t[:, g * 512:(g + 1) * 512], op=ALU.mult),
                    deps=[pmm, t_sf])
                rope_done = sch.op("vector", lambda e, buf=buf, sl=sl: e.tensor_tensor(
                    out=buf[:, sl], in0=ropescr[:, 0:512],
                    in1=ropescr[:, 512:1024], op=ALU.add),
                    deps=[v1, v2])
                pd_free = v2

            for tt in range(NT):
                mmtok = proj_tile(PC, "PC", 0, wv_ok, 2, tt)
                vview = va3[:, tt]
                vevac = sch.op("scalar", lambda e, vview=vview: e.copy(
                    out=vview[:, :, 0:64],
                    in_=PC[:, :].rearrange("p (h c) -> p h c", c=64)),
                    deps=[mmtok, va_ones])
                ps_evac["PC"] = vevac
                for u in range(4):
                    buf, c, g = rope_units[4 * tt + u]
                    emit_rope(buf, c, g)

            if s == 0 and "qfm" in dbg:
                out_toks.append(sch.dma("gpsimd", dbg["qfm"][:], q_fm, deps=[rope_done]))
            if s == 0 and "kfm" in dbg:
                out_toks.append(sch.dma("gpsimd", dbg["kfm"][:], k_fm, deps=[rope_done]))
            if s == 0 and "vaug" in dbg:
                out_toks.append(sch.dma("gpsimd", dbg["vaug"][:], v_aug, deps=[vevac]))

            ar.reset(m_A)
            sch.barrier()

            m_B = ar.mark()
            wo_sb = ar.bf16(128, NCH * DM)
            pT = ar.bf16(128, NT * 1024)
            ao_fm = ar.bf16(128, NCH * 512)
            rb_sb = ar.bf16(64, 2 * 512)
            rrow = ar.bf16(1, 2 * 512)
            z1 = ar.f32(128, 1024)
            x_t = ar.f32(128, 2 * 1024)
            x1o = ar.f32(128, 2 * 1024)
            x1hi_t = ar.bf16(128, 1024)
            x1lo_t = ar.bf16(128, 1024)
            tstage = ar.bf16(128, 2 * 1024)

            wo_ok = load_w(d_woT, wo_sb)

            xdma = [None] * NT
            x_t_free = [None, None]
            xdma[0] = sch.dma("gpsimd", x_t[:, 0:1024], d_xtm[s, 0:128, :])

            R = lambda i: rsc[:, i * NE:(i + 1) * NE]
            (lg, e1b, dnm, inh, tC, dist, m1b, zone, wv_, fbb, selb, lg2) = (
                R(0), R(1), R(2), R(3), R(4), R(5), R(6), R(7), R(8), R(9),
                R(10), R(11))
            w_done = None
            tstage_rfree = None
            pending_norm = []

            def flush_norm():
                nonlocal pd_freeB, aoev_last
                while pending_norm:
                    p0, h, rc, stop_tok = pending_norm.pop(0)
                    rb = sch.op("tensor", lambda e, h=h: e.matmul(
                        out=PD[0:64, :], lhsT=ones1[0:1, 0:64],
                        rhs=rrow[0:1, h * 512:(h + 1) * 512],
                        start=True, stop=True), deps=[rc, pd_freeB])
                    rrow_free[h] = rb
                    rbe = sch.op("vector", lambda e, h=h: e.tensor_copy(
                        out=rb_sb[0:64, h * 512:(h + 1) * 512],
                        in_=PD[0:64, :]), deps=[rb, rbsb_free[h]])
                    pd_freeB = rbe
                    ao = sch.op("vector", lambda e, h=h, p0=p0: e.tensor_tensor(
                        out=ao_fm[h * 64:(h + 1) * 64, p0 * 512:(p0 + 1) * 512],
                        in0=PC[0:64, h * 512:(h + 1) * 512],
                        in1=rb_sb[0:64, h * 512:(h + 1) * 512],
                        op=ALU.mult), deps=[rbe, stop_tok, aofm_free])
                    pc_free[h] = ao
                    rbsb_free[h] = ao
                    aoev_last = ao
            av_tok = [None] * 4
            lastS = {"PA": None, "PB": None}
            pc_free = [None, None]
            pd_freeB = None
            rrow_free = [None, None]
            rbsb_free = [None, None]
            aofm_free = None
            aoev_last = None
            x1o_free = [None, None]
            tstage_free = [None, None]
            hilo_free = [None, None]
            z1_free = None

            for slb in range(NSL):
                q0 = slb * 512
                pre_mm = {}

                def emit_smm(p2, kt, pfree):
                    P = PA if kt % 2 == 0 else PB
                    mmtok = None
                    for h in range(2):
                        b0 = h * 64
                        mmtok = sch.op("tensor", lambda e, P=P, p2=p2, h=h, kt=kt, q0=q0, b0=b0: e.matmul(
                            out=P[:, h * 512:(h + 1) * 512],
                            lhsT=k_fm[b0:b0 + 64, p2 * L + kt * 128: p2 * L + kt * 128 + 128],
                            rhs=q_fm[b0:b0 + 64, p2 * L + q0: p2 * L + q0 + 512],
                            start=True, stop=True),
                            deps=([pfree, rope_done] if h == 0 else ()),
                            inc=(h == 1))
                    return mmtok

                for p in range(NH // 2):
                    exps = [None] * NT
                    av_new = [None] * 4
                    last_av = None

                    def emit_av(kc, p=p):
                        nonlocal last_av
                        for h in range(2):
                            hg = 2 * p + h
                            t = sch.op("tensor", lambda e, h=h, hg=hg, kc=kc: e.matmul(
                                out=PC[0:65, h * 512:(h + 1) * 512],
                                lhsT=v_aug[:, kc * NH * 65 + hg * 65: kc * NH * 65 + hg * 65 + 65],
                                rhs=pT[:, kc * 1024 + h * 512: kc * 1024 + (h + 1) * 512],
                                start=(kc == 0), stop=(kc == NT - 1)),
                                deps=([exps[kc], pc_free[h], vevac]
                                      if kc == 0
                                      else ([exps[kc]] if h == 0 else ())),
                                inc=(kc % 4 == 3 and h == 1))
                            if kc % 4 == 3 and h == 1:
                                av_new[kc // 4] = t
                                last_av = t

                    for kt in range(NT):
                        if kt == 4:
                            flush_norm()
                        pname = "PA" if kt % 2 == 0 else "PB"
                        P = PA if kt % 2 == 0 else PB
                        if kt in pre_mm:
                            mmtok = pre_mm.pop(kt)
                        else:
                            pfree = exps[kt - 2] if kt >= 2 else lastS[pname]
                            mmtok = emit_smm(p, kt, pfree)
                        exps[kt] = sch.op("scalar", lambda e, P=P, kt=kt: e.activation(
                            out=pT[:, kt * 1024:(kt + 1) * 1024], in_=P[:],
                            func=AF.Exp, scale=0.125),
                            deps=[mmtok, av_tok[min(3, kt // 4)]])
                        if kt >= 6:
                            emit_av(kt - 6)
                    lastS["PA"] = exps[NT - 2]
                    lastS["PB"] = exps[NT - 1]
                    for kc in range(10, NT):
                        emit_av(kc)
                        if kc >= 14 and p < NH // 2 - 1:
                            kt0 = kc - 14
                            pre_mm[kt0] = emit_smm(p + 1, kt0, exps[14 + kt0])
                    av_tok = av_new
                    for h in range(2):
                        def _rc(e, h=h):
                            with nc.allow_low_precision("softmax 1/sum feeds a bf16 multiply"):
                                return e.reciprocal(
                                    out=rrow[0:1, h * 512:(h + 1) * 512],
                                    in_=PC[64:65, h * 512:(h + 1) * 512])
                        rc = sch.op("vector", _rc, deps=[last_av, rrow_free[h]])
                        pending_norm.append((p, h, rc, last_av))

                if s == 0 and slb == 0 and "pt" in dbg:
                    out_toks.append(sch.dma("gpsimd", dbg["pt"][:], pT, deps=[aoev_last]))
                if s == 0 and slb == 0 and "ao" in dbg:
                    out_toks.append(sch.dma("gpsimd", dbg["ao"][:], ao_fm, deps=[aoev_last]))

                flush_norm()
                oproj_last = None
                for j in range(4):
                    tt = slb * 4 + j
                    P = PA if tt % 2 == 0 else PB
                    pname = "PA" if tt % 2 == 0 else "PB"
                    if tt + 1 < NT:
                        nb = (tt + 1) % 2
                        xdma[tt + 1] = sch.dma(
                            "gpsimd", x_t[:, nb * 1024:(nb + 1) * 1024],
                            d_xtm[s, (tt + 1) * 128:(tt + 2) * 128, :],
                            deps=[x_t_free[nb]])
                    deps0 = [aoev_last, wo_ok, lastS[pname]]
                    for c in range(NCH):
                        for half in range(2):
                            sch.op("tensor", lambda e, P=P, c=c, half=half, j=j: e.matmul(
                                out=P[:, half * 512:(half + 1) * 512],
                                lhsT=ao_fm[:, c * 512 + j * 128: c * 512 + j * 128 + 128],
                                rhs=wo_sb[:, c * DM + half * 512: c * DM + (half + 1) * 512],
                                start=(c == 0), stop=False),
                                deps=(deps0 if c == 0 and half == 0 else ()),
                                inc=False)
                    mmtok = None
                    for half in range(2):
                        mmtok = sch.op("tensor", lambda e, P=P, half=half: e.matmul(
                            out=P[:, half * 512:(half + 1) * 512],
                            lhsT=ones1[:],
                            rhs=browb[0:1, 3 * DM + half * 512: 3 * DM + (half + 1) * 512],
                            start=False, stop=True), inc=(half == 1))
                    if j == 3:
                        oproj_last = mmtok
                    xb = tt % 2
                    xsl = slice(xb * 1024, (xb + 1) * 1024)
                    v1 = sch.op("vector", lambda e, P=P, xsl=xsl: e.tensor_tensor(
                        out=z1[:], in0=P[:], in1=x_t[:, xsl], op=ALU.add),
                        deps=[mmtok, xdma[tt], z1_free])
                    lastS[pname] = v1
                    v2 = sch.op("vector", lambda e: e.reduce_sum(
                        out=stats[:, 0:1], in_=z1[:], axis=AX.X), deps=[v1])
                    s1 = sch.op("scalar", lambda e, xsl=xsl: e.activation(
                        out=x_t[:, xsl], in_=z1[:], func=AF.Square,
                        accum_out=stats[:, 1:2]), deps=[v1])
                    x_t_free[xb] = s1
                    v3 = sch.op("vector", lambda e: e.tensor_scalar(
                        out=stats[:, 2:3], in0=stats[:, 0:1], scalar1=-1.0 / DM,
                        scalar2=None, op0=ALU.mult), deps=[v2])
                    v4 = sch.op("vector", lambda e: e.tensor_tensor(
                        out=stats[:, 3:4], in0=stats[:, 2:3], in1=stats[:, 2:3],
                        op=ALU.mult), deps=[v3])
                    v5 = sch.op("vector", lambda e: e.tensor_scalar(
                        out=stats[:, 4:5], in0=stats[:, 1:2], scalar1=1.0 / DM,
                        scalar2=None, op0=ALU.mult), deps=[s1])
                    v6 = sch.op("vector", lambda e: e.tensor_tensor(
                        out=stats[:, 5:6], in0=stats[:, 4:5], in1=stats[:, 3:4],
                        op=ALU.subtract), deps=[v4, v5])
                    s2 = sch.op("scalar", lambda e: e.activation(
                        out=stats[:, 6:7], in_=stats[:, 5:6], func=AF.Ln,
                        bias=epscol[:]), deps=[v6])
                    s3 = sch.op("scalar", lambda e: e.activation(
                        out=stats[:, 7:8], in_=stats[:, 6:7], func=AF.Exp,
                        scale=-0.5), deps=[s2])
                    ob = tt % 2
                    osl = slice(ob * 1024, (ob + 1) * 1024)
                    v7 = sch.op("vector", lambda e, osl=osl: e.tensor_scalar(
                        out=x1o[:, osl], in0=z1[:], scalar1=stats[:, 2:3],
                        scalar2=stats[:, 7:8], op0=ALU.add, op1=ALU.mult),
                        deps=[v1, v3, s3, x1o_free[ob]])
                    s4 = sch.op("scalar", lambda e, osl=osl: e.copy(
                        out=x1hi_t[:], in_=x1o[:, osl]), deps=[v7, hilo_free[0]])
                    v8 = sch.op("vector", lambda e, osl=osl: e.tensor_tensor(
                        out=x1lo_t[:], in0=x1o[:, osl], in1=x1hi_t[:],
                        op=ALU.subtract), deps=[v7, s4, hilo_free[1]])
                    z1_free = v7
                    ptes = []
                    for which, (buf, tok_in) in enumerate(
                            ((x1hi_t, s4), (x1lo_t, v8))):
                        hsl = slice(which * 1024, (which + 1) * 1024)
                        ttr = None
                        for c in range(NCH):
                            ttr = sch.op("tensor", lambda e, buf=buf, c=c: e.transpose(
                                out=PT[:, c * 128:(c + 1) * 128],
                                in_=buf[:, c * 128:(c + 1) * 128],
                                identity=identb[:]),
                                deps=([tok_in, psfree["PT"]] if c == 0 else ()),
                                inc=(c == NCH - 1))
                        hilo_free[which] = ttr
                        pte = sch.op("scalar", lambda e, hsl=hsl: e.copy(
                            out=tstage[:, hsl], in_=PT[:]),
                            deps=[ttr, tstage_free[which], tstage_rfree])
                        psfree["PT"] = pte
                        ptes.append(pte)
                    dview = d_x1hi.rearrange("(c p) t -> p c t", p=128)
                    tdo = sch.dma("gpsimd",
                                  dview[:, :, tt * 128:(tt + 1) * 128],
                                  tstage[:, 0:1024].rearrange("p (c t) -> p c t", t=128),
                                  deps=[ptes[0]])
                    tstage_free[0] = tdo
                    tdx = sch.dma("gpsimd", d_x1tm[tt * 128:(tt + 1) * 128, :],
                                  x1o[:, osl], deps=[v7, s4, v8])
                    x1o_free[ob] = tdx
                    for c in range(NCH):
                        sch.op("tensor", lambda e, c=c: e.matmul(
                            out=PD[:, 0:16],
                            lhsT=tstage[:, c * 128:(c + 1) * 128],
                            rhs=whilo[:, c * 16:(c + 1) * 16],
                            start=(c == 0), stop=(c == NCH - 1)),
                            deps=([ptes[0], ptes[1], pd_freeB] if c == 0 else ()),
                            inc=False)
                    rmm = None
                    for c in range(NCH):
                        rmm = sch.op("tensor", lambda e, c=c: e.matmul(
                            out=PD[:, 16:24],
                            lhsT=tstage[:, 1024 + c * 128: 1024 + (c + 1) * 128],
                            rhs=whilo[:, c * 16: c * 16 + 8],
                            start=(c == 0), stop=(c == NCH - 1)),
                            inc=(c == NCH - 1))
                    tstage_rfree = rmm
                    sch.op("vector", lambda e: e.tensor_copy(
                        out=lg2[:], in_=PD[:, 0:8]), deps=[rmm])
                    sch.op("vector", lambda e: e.tensor_tensor(
                        out=lg2[:], in0=lg2[:], in1=PD[:, 8:16], op=ALU.add))
                    pd_freeB = sch.op("vector", lambda e: e.tensor_tensor(
                        out=lg[:], in0=lg2[:], in1=PD[:, 16:24], op=ALU.add))
                    sch.op("vector", lambda e: e.tensor_scalar(
                        out=dnm[:], in0=lg[:], scalar1=invT[:, 0:1], scalar2=-1.0,
                        op0=ALU.mult, op1=ALU.mult))
                    t_e1 = sch.op("scalar", lambda e: e.activation(
                        out=e1b[:], in_=dnm[:], func=AF.Exp),
                        deps=[sch.lasttok["vector"]])
                    sch.op("vector", lambda e: e.tensor_scalar(
                        out=dnm[:], in0=e1b[:], scalar1=1.0, scalar2=None,
                        op0=ALU.add), deps=[t_e1])
                    sch.op("vector", lambda e: e.reciprocal(out=inh[:], in_=dnm[:]))
                    sch.op("vector", lambda e: e.tensor_scalar(
                        out=tC[:], in0=inh[:], scalar1=-GOLD_C, scalar2=None,
                        op0=ALU.add))
                    t_ab = sch.op("scalar", lambda e: e.activation(
                        out=dist[:], in_=tC[:], func=AF.Abs),
                        deps=[sch.lasttok["vector"]])
                    sch.op("vector", lambda e: e.tensor_scalar(
                        out=m1b[:], in0=inh[:], scalar1=GOLD_L, scalar2=None,
                        op0=ALU.is_ge))
                    sch.op("vector", lambda e: e.tensor_scalar(
                        out=zone[:], in0=inh[:], scalar1=GOLD_U, scalar2=None,
                        op0=ALU.is_le))
                    sch.op("vector", lambda e: e.tensor_tensor(
                        out=zone[:], in0=zone[:], in1=m1b[:], op=ALU.mult))
                    t_ew = sch.op("scalar", lambda e: e.activation(
                        out=wv_[:], in_=dist[:], func=AF.Exp, scale=-10.0),
                        deps=[t_ab, sch.lasttok["vector"]])
                    t_fb = sch.op("scalar", lambda e: e.activation(
                        out=fbb[:], in_=dist[:], func=AF.Exp, scale=-10.0 / 3.0),
                        deps=[t_ew])
                    sch.op("vector", lambda e: e.tensor_tensor(
                        out=wv_[:], in0=wv_[:], in1=zone[:], op=ALU.mult),
                        deps=[t_ew])
                    sch.op("vector", lambda e: e.reduce_sum(
                        out=stats[:, 8:9], in_=wv_[:], axis=AX.X))
                    sch.op("vector", lambda e: e.tensor_scalar(
                        out=stats[:, 9:10], in0=stats[:, 8:9], scalar1=1e-8,
                        scalar2=None, op0=ALU.is_lt))
                    sch.op("vector", lambda e: e.reduce_max(
                        out=stats[:, 10:11], in_=fbb[:], axis=AX.X), deps=[t_fb])
                    sch.op("vector", lambda e: e.tensor_scalar(
                        out=m1b[:], in0=fbb[:], scalar1=stats[:, 10:11],
                        scalar2=None, op0=ALU.is_ge))
                    sch.op("vector", lambda e: e.tensor_scalar(
                        out=selb[:], in0=m1b[:], scalar1=-1.0, scalar2=1.0,
                        op0=ALU.mult, op1=ALU.add))
                    sch.op("vector", lambda e: e.tensor_tensor(
                        out=selb[:], in0=fbb[:], in1=selb[:], op=ALU.mult))
                    sch.op("vector", lambda e: e.reduce_max(
                        out=stats[:, 11:12], in_=selb[:], axis=AX.X))
                    sch.op("vector", lambda e: e.tensor_scalar(
                        out=selb[:], in0=selb[:], scalar1=stats[:, 11:12],
                        scalar2=None, op0=ALU.is_ge))
                    sch.op("vector", lambda e: e.tensor_tensor(
                        out=m1b[:], in0=m1b[:], in1=selb[:], op=ALU.add))
                    sch.op("vector", lambda e: e.tensor_tensor(
                        out=fbb[:], in0=fbb[:], in1=m1b[:], op=ALU.mult))
                    sch.op("vector", lambda e: e.reduce_sum(
                        out=stats[:, 12:13], in_=fbb[:], axis=AX.X))
                    sch.op("vector", lambda e: e.tensor_scalar_max(
                        out=stats[:, 12:13], in0=stats[:, 12:13], scalar1=1e-8))
                    sch.op("vector", lambda e: e.reciprocal(
                        out=stats[:, 13:14], in_=stats[:, 12:13]))
                    sch.op("vector", lambda e: e.tensor_scalar(
                        out=stats[:, 14:15], in0=stats[:, 9:10], scalar1=-1.0,
                        scalar2=1.0, op0=ALU.mult, op1=ALU.add))
                    sch.op("vector", lambda e: e.tensor_scalar(
                        out=wv_[:], in0=wv_[:], scalar1=stats[:, 14:15],
                        scalar2=None, op0=ALU.mult))
                    sch.op("vector", lambda e: e.tensor_scalar(
                        out=fbb[:], in0=fbb[:], scalar1=stats[:, 13:14],
                        scalar2=None, op0=ALU.mult))
                    sch.op("vector", lambda e: e.tensor_scalar(
                        out=fbb[:], in0=fbb[:], scalar1=stats[:, 9:10],
                        scalar2=None, op0=ALU.mult))
                    sch.op("vector", lambda e: e.tensor_tensor(
                        out=wv_[:], in0=wv_[:], in1=fbb[:], op=ALU.add))
                    sch.op("vector", lambda e: e.reduce_sum(
                        out=stats[:, 15:16], in_=wv_[:], axis=AX.X))
                    sch.op("vector", lambda e: e.tensor_scalar_max(
                        out=stats[:, 15:16], in0=stats[:, 15:16], scalar1=1e-8))
                    sch.op("vector", lambda e: e.reciprocal(
                        out=stats[:, 8:9], in_=stats[:, 15:16]))
                    w_done = sch.op("vector", lambda e, tt=tt: e.tensor_scalar(
                        out=w_tm[:, tt * NE:(tt + 1) * NE], in0=wv_[:],
                        scalar1=stats[:, 8:9], scalar2=None, op0=ALU.mult))
                aofm_free = oproj_last

            ar.reset(m_span)
            sch.barrier()

            acc = ar.bf16(128, 2 * 8 * 1024)
            x1t_in = ar.f32(128, 2 * 1024)
            outst = ar.f32(128, 2 * 1024)
            z2 = ar.f32(128, 2 * 1024)
            x1hi_fm = ar.bf16(128, NCH * L)
            t_hi = sch.dma("sync", x1hi_fm.rearrange("p (c t) -> p c t", t=L),
                           d_x1hi.rearrange("(c p) t -> p c t", p=128))
            if s == 0 and "x1hi" in dbg:
                out_toks.append(sch.dma("gpsimd", dbg["x1hi"][:],
                                        d_x1hi[:], deps=[t_hi]))
            if s == 0 and "x1tmv" in dbg:
                out_toks.append(sch.dma("gpsimd", dbg["x1tmv"][:], d_x1tm[:]))
            if s == 0 and "w_tm" in dbg:
                out_toks.append(sch.dma("gpsimd", dbg["w_tm"][:], w_tm[:],
                                        deps=[w_done]))

            w_fm = ar.bf16(NE, NT * 128)
            wpoolF = [ar.bf16(128, NCH * DM) for _ in range(4)]
            hw = ar.bf16(128, NCH * 512)
            sg = ar.f32(128, 512)
            h_sb = ar.bf16(128, 512)
            wb_sb = ar.bf16(128, 512)

            wfm_ev = None
            for tt in range(NT):
                wtr = sch.op("tensor", lambda e, tt=tt: e.transpose(
                    out=PT[0:NE, 0:128], in_=w_tm[:, tt * NE:(tt + 1) * NE],
                    identity=identb[:]),
                    deps=[w_done, psfree["PT"]])
                wfm_ev = sch.op("vector", lambda e, tt=tt: e.tensor_copy(
                    out=w_fm[0:NE, tt * 128:(tt + 1) * 128],
                    in_=PT[0:NE, 0:128]), deps=[wtr])
                psfree["PT"] = wfm_ev
            wfm_ok = wfm_ev

            wslot_free = [None] * 4
            slotctr = [0]

            def load_wffn(dram2d):
                slot = slotctr[0] % 4
                slotctr[0] += 1
                tok = sch.dma("gpsimd",
                              wpoolF[slot].rearrange("p (c i) -> p c i", i=DM),
                              dram2d.rearrange("(c p) i -> p c i", p=128),
                              deps=[wslot_free[slot]])
                return slot, tok

            gu_free = [None, None]
            pc_freeF = [None, None]
            pd_wb = pd_freeB
            hw_free = None
            wb_free = None
            sg_free = None
            hsb_free = None
            acc_free = [None] * 8
            acc_tok = [None] * 8
            x1t_free = [None, None]
            out_free = [None, None]
            ffn_frontier = [(s3n, v3n) for s3n, v3n in sch.cnt.items() if v3n > 0]
            preW = {}

            def get_w(wp2, e2):
                if (wp2, e2) in preW:
                    return preW.pop((wp2, e2))
                return (load_wffn(d_wg[e2]), load_wffn(d_wu[e2]),
                        load_wffn(d_wd[e2]))

            for wpass in range(2):
                for e_i in range(NE):
                    (slot_g, tg), (slot_u, tu), (slot_d, td_) = get_w(wpass, e_i)
                    for sli in range(2):
                        slb2 = 2 * wpass + sli
                        q0 = slb2 * 512
                        wbmm = sch.op("tensor", lambda e, e_i=e_i, slb2=slb2: e.matmul(
                            out=PD[:], lhsT=onehb[:, e_i * 128:(e_i + 1) * 128],
                            rhs=w_fm[0:NE, slb2 * 512:(slb2 + 1) * 512],
                            start=True, stop=True), deps=[wfm_ok, pd_wb])
                        wbe = sch.op("scalar", lambda e: e.copy(
                            out=wb_sb[:], in_=PD[:]), deps=[wbmm, wb_free])
                        pd_wb = wbe
                        hw_last = None
                        for it in range(NCH):
                            gdep = [tg, gu_free[0], t_hi]
                            gtok = None
                            for c in range(NCH):
                                gtok = sch.op("tensor", lambda e, c=c, it=it, q0=q0, slot_g=slot_g: e.matmul(
                                    out=PA[:, 0:512],
                                    lhsT=wpoolF[slot_g][:, c * DM + it * 128: c * DM + it * 128 + 128],
                                    rhs=x1hi_fm[:, c * L + q0: c * L + q0 + 512],
                                    start=(c == 0), stop=(c == NCH - 1)),
                                    deps=(gdep if c == 0 else ()),
                                    inc=(c == NCH - 1))
                            udep = [tu, gu_free[1]]
                            utok = None
                            for c in range(NCH):
                                utok = sch.op("tensor", lambda e, c=c, it=it, q0=q0, slot_u=slot_u: e.matmul(
                                    out=PA[:, 512:1024],
                                    lhsT=wpoolF[slot_u][:, c * DM + it * 128: c * DM + it * 128 + 128],
                                    rhs=x1hi_fm[:, c * L + q0: c * L + q0 + 512],
                                    start=(c == 0), stop=(c == NCH - 1)),
                                    deps=(udep if c == 0 else ()),
                                    inc=(c == NCH - 1))
                            sgt = sch.op("scalar", lambda e: e.activation(
                                out=sg[:], in_=PA[:, 0:512], func=AF.Silu),
                                deps=[gtok, sg_free])
                            gu_free[0] = sgt
                            ht = sch.op("vector", lambda e: e.tensor_tensor(
                                out=h_sb[:], in0=sg[:], in1=PA[:, 512:1024],
                                op=ALU.mult), deps=[sgt, utok, hsb_free])
                            sg_free = ht
                            gu_free[1] = ht
                            hwt = sch.op("vector", lambda e, it=it: e.tensor_tensor(
                                out=hw[:, it * 512:(it + 1) * 512], in0=h_sb[:],
                                in1=wb_sb[:], op=ALU.mult),
                                deps=[ht, wbe, hw_free])
                            hsb_free = hwt
                            hw_last = hwt
                        wb_free = hw_last
                        for jj in range(4):
                            ttp = sli * 4 + jj
                            Pd2 = PC if jj % 2 == 0 else PB
                            dep0 = [td_, hw_last, pc_freeF[jj % 2]]
                            dtok = None
                            for ic in range(NCH):
                                for half in range(2):
                                    dtok = sch.op("tensor", lambda e, Pd2=Pd2, ic=ic, half=half, jj=jj, slot_d=slot_d: e.matmul(
                                        out=Pd2[:, half * 512:(half + 1) * 512],
                                        lhsT=hw[:, ic * 512 + jj * 128: ic * 512 + jj * 128 + 128],
                                        rhs=wpoolF[slot_d][:, ic * DM + half * 512: ic * DM + (half + 1) * 512],
                                        start=(ic == 0), stop=(ic == NCH - 1)),
                                        deps=(dep0 if ic == 0 and half == 0 else ()),
                                        inc=(ic == NCH - 1 and half == 1))
                            ao2 = (wpass % 2) * 8192 + ttp * 1024
                            if e_i == 0:
                                def _ev0(e, ao2=ao2, Pd2=Pd2):
                                    with nc.allow_low_precision("expert partial sums kept in bf16"):
                                        return e.tensor_copy(
                                            out=acc[:, ao2:ao2 + 1024],
                                            in_=Pd2[:])
                                ev = sch.op("vector", _ev0, deps=[dtok])
                            else:
                                def _eva(e, ao2=ao2, Pd2=Pd2):
                                    with nc.allow_low_precision("expert partial sums kept in bf16"):
                                        return e.tensor_tensor(
                                            out=acc[:, ao2:ao2 + 1024],
                                            in0=acc[:, ao2:ao2 + 1024],
                                            in1=Pd2[:], op=ALU.add)
                                ev = sch.op("vector", _eva, deps=[dtok])
                            acc_tok[ttp] = ev
                            pc_freeF[jj % 2] = ev
                        hw_free = dtok
                    wslot_free[slot_g] = sch.ttok()
                    wslot_free[slot_u] = sch.ttok()
                    wslot_free[slot_d] = sch.ttok()

                if wpass == 0:
                    preW[(1, 0)] = (load_wffn(d_wg[0]), load_wffn(d_wu[0]),
                                    load_wffn(d_wd[0]))
                if wpass == 1:
                    ffn_core = [sch.ttok(), sch.lasttok["vector"],
                                sch.lasttok["scalar"]]
                for sli in range(2):
                    slb2 = 2 * wpass + sli
                    for jj0 in (0, 2):
                        pair = []
                        for jj in (jj0, jj0 + 1):
                            tt = slb2 * 4 + jj
                            pair.append(dict(
                                tt=tt, ttp=sli * 4 + jj, xb=tt % 2,
                                bk=8 * (jj % 2),
                                xsl=slice((tt % 2) * 1024, (tt % 2 + 1) * 1024),
                                zsl=slice((jj % 2) * 1024, (jj % 2 + 1) * 1024),
                                asl=slice((wpass % 2) * 8192 + (sli * 4 + jj) * 1024,
                                          (wpass % 2) * 8192 + (sli * 4 + jj + 1) * 1024)))
                        for ti in pair:
                            ti["txd"] = sch.dma(
                                "gpsimd", x1t_in[:, ti["xsl"]],
                                d_x1tm[ti["tt"] * 128:(ti["tt"] + 1) * 128, :],
                                deps=([x1t_free[ti["xb"]]]
                                      if x1t_free[ti["xb"]] is not None
                                      else ffn_frontier))
                        for ti in pair:
                            ti["v1"] = sch.op("vector", lambda e, ti=ti: e.tensor_tensor(
                                out=z2[:, ti["zsl"]], in0=acc[:, ti["asl"]],
                                in1=x1t_in[:, ti["xsl"]], op=ALU.add),
                                deps=[acc_tok[ti["ttp"]], ti["txd"]])
                            x1t_free[ti["xb"]] = ti["v1"]
                        for ti in pair:
                            bk = ti["bk"]
                            ti["v2"] = sch.op("vector", lambda e, ti=ti, bk=bk: e.reduce_sum(
                                out=stats[:, bk:bk + 1], in_=z2[:, ti["zsl"]],
                                axis=AX.X), deps=[ti["v1"]])
                            ti["s1"] = sch.op("scalar", lambda e, ti=ti, bk=bk: e.activation(
                                out=acc[:, ti["asl"]], in_=z2[:, ti["zsl"]],
                                func=AF.Square,
                                accum_out=stats[:, bk + 1:bk + 2]), deps=[ti["v1"]])
                            acc_free[ti["ttp"]] = ti["s1"]
                        for ti in pair:
                            bk = ti["bk"]
                            ti["v3"] = sch.op("vector", lambda e, bk=bk: e.tensor_scalar(
                                out=stats[:, bk + 2:bk + 3], in0=stats[:, bk:bk + 1],
                                scalar1=-1.0 / DM, scalar2=None, op0=ALU.mult),
                                deps=[ti["v2"]])
                        for ti in pair:
                            bk = ti["bk"]
                            ti["v4"] = sch.op("vector", lambda e, bk=bk: e.tensor_tensor(
                                out=stats[:, bk + 3:bk + 4], in0=stats[:, bk + 2:bk + 3],
                                in1=stats[:, bk + 2:bk + 3], op=ALU.mult),
                                deps=[ti["v3"]])
                        for ti in pair:
                            bk = ti["bk"]
                            ti["v5"] = sch.op("vector", lambda e, bk=bk: e.tensor_scalar(
                                out=stats[:, bk + 4:bk + 5], in0=stats[:, bk + 1:bk + 2],
                                scalar1=1.0 / DM, scalar2=None, op0=ALU.mult),
                                deps=[ti["s1"]])
                        for ti in pair:
                            bk = ti["bk"]
                            ti["v6"] = sch.op("vector", lambda e, bk=bk: e.tensor_tensor(
                                out=stats[:, bk + 5:bk + 6], in0=stats[:, bk + 4:bk + 5],
                                in1=stats[:, bk + 3:bk + 4], op=ALU.subtract),
                                deps=[ti["v4"], ti["v5"]])
                        for ti in pair:
                            bk = ti["bk"]
                            ti["s2"] = sch.op("scalar", lambda e, bk=bk: e.activation(
                                out=stats[:, bk + 6:bk + 7], in_=stats[:, bk + 5:bk + 6],
                                func=AF.Ln, bias=epscol[:]), deps=[ti["v6"]])
                        for ti in pair:
                            bk = ti["bk"]
                            ti["s3"] = sch.op("scalar", lambda e, bk=bk: e.activation(
                                out=stats[:, bk + 7:bk + 8], in_=stats[:, bk + 6:bk + 7],
                                func=AF.Exp, scale=-0.5), deps=[ti["s2"]])
                        for ti in pair:
                            bk = ti["bk"]
                            ob = ti["tt"] % 2
                            osl2 = slice(ob * 1024, (ob + 1) * 1024)
                            v7 = sch.op("vector", lambda e, ti=ti, bk=bk, osl2=osl2:
                                        e.tensor_scalar(
                                out=outst[:, osl2], in0=z2[:, ti["zsl"]],
                                scalar1=stats[:, bk + 2:bk + 3],
                                scalar2=stats[:, bk + 7:bk + 8],
                                op0=ALU.add, op1=ALU.mult),
                                deps=[ti["v1"], ti["v3"], ti["s3"], out_free[ob]])
                            tdo = sch.dma("gpsimd",
                                          d_out[s, ti["tt"] * 128:(ti["tt"] + 1) * 128, :],
                                          outst[:, osl2], deps=[v7])
                            out_free[ob] = tdo
                            out_toks.append(tdo)

            ar.reset(m_span)
            if s + 1 < SEQ:
                f = [(sname, v) for sname, v in sch.cnt.items() if v > 0]
                for e2 in ("tensor", "scalar", "vector"):
                    sch.pending[e2].extend(f)
                for e2 in ("sync", "gpsimd"):
                    sch.pending[e2].extend(ffn_core)
            else:
                sch.barrier()

        sch.wait_raw("gpsimd", [("DG", sch.cnt["DG"])])
        sch.wait_raw("sync", [("DS", sch.cnt["DS"])])
        sch.finalize(block, sems)
    return nc


def _host_consts():
    inv = 1.0 / (10000.0 ** (np.arange(0, HD, 2, dtype=np.float64) / HD))
    t = np.arange(L, dtype=np.float64)[:, None]
    fr = t * inv[None, :]
    cos = np.cos(fr)
    sin = np.sin(fr)
    cfm = np.empty((128, L), np.float64)
    sfm = np.empty((128, L), np.float64)
    for p in range(128):
        i = p % HD
        cfm[p] = cos[:, i % 32]
        sfm[p] = (-1.0 if i < 32 else 1.0) * sin[:, i % 32]
    perm = np.zeros((128, 128), np.float32)
    for m in range(128):
        i = m % HD
        k = m + 32 if i < 32 else m - 32
        perm[k, m] = 1.0
    oneh = np.zeros((NE, NE * 128), np.float32)
    for e in range(NE):
        oneh[e, e * 128:(e + 1) * 128] = 1.0
    return (np.ascontiguousarray(cfm.astype(np.float32)),
            np.ascontiguousarray(sfm.astype(np.float32)),
            perm, np.eye(128, dtype=np.float32), oneh)


_PROG_CACHE = {}


def kernel(**inputs):
    x = np.ascontiguousarray(np.asarray(inputs["L0_attn_x"], dtype=np.float32))
    Wq = np.asarray(inputs["L0_attn_Wq"], np.float32)
    bq = np.asarray(inputs["L0_attn_bq"], np.float32)
    Wk = np.asarray(inputs["L0_attn_Wk"], np.float32)
    bk = np.asarray(inputs["L0_attn_bk"], np.float32)
    Wv = np.asarray(inputs["L0_attn_Wv"], np.float32)
    bv = np.asarray(inputs["L0_attn_bv"], np.float32)
    Wo = np.asarray(inputs["L0_attn_Wo"], np.float32)
    bo = np.asarray(inputs["L0_attn_bo"], np.float32)
    Wg = np.ascontiguousarray(np.asarray(inputs["L0_ffn_Wg"], np.float32))
    Wu = np.ascontiguousarray(np.asarray(inputs["L0_ffn_Wu"], np.float32))
    Wd = np.ascontiguousarray(np.asarray(inputs["L0_ffn_Wd"], np.float32))
    Wr = np.ascontiguousarray(np.asarray(inputs["L0_ffn_Wr"], np.float32))
    temp = np.asarray(inputs["L0_ffn_temperature"], np.float32)

    cfm, sfm, perm, ident, oneh = _host_consts()
    wqT = np.ascontiguousarray(Wq.T)
    wkT = np.ascontiguousarray(Wk.T)
    wvT = np.ascontiguousarray(Wv.T)
    woT = np.ascontiguousarray(Wo.T)
    tbc = np.ascontiguousarray(np.broadcast_to(temp.reshape(1, 1), (128, 1)))

    key = tuple(DEBUG)
    if key not in _PROG_CACHE:
        _PROG_CACHE[key] = build_program()
    nc = _PROG_CACHE[key]

    xs = x.reshape(NCORES, SEQ, L, DM)
    in_maps = []
    for c in range(NCORES):
        xc = np.ascontiguousarray(xs[c])
        xTc = np.ascontiguousarray(xc.transpose(0, 2, 1))
        in_maps.append(dict(
            xtm=xc, xfm=xTc, wqT=wqT, wkT=wkT, wvT=wvT, woT=woT,
            bq=bq.reshape(1, DM), bk=bk.reshape(1, DM),
            bv=bv.reshape(1, DM), bo=bo.reshape(1, DM),
            wg=Wg, wu=Wu, wd=Wd, wr=Wr, tbc=tbc,
            cfm=cfm, sfm=sfm, ident=ident, permm=perm, oneh=oneh))

    res = run_bass_kernel_spmd(nc, in_maps, core_ids=list(range(NCORES)))
    out = np.concatenate(
        [np.asarray(r["out"]).reshape(SEQ, L, DM) for r in res.results], axis=0)
    if DEBUG:
        kernel.dbg = [{k: np.asarray(v) for k, v in r.items()
                       if k.startswith("dbg_")} for r in res.results]
    return out.reshape(16, L, DM)
